# version 02
# baseline (speedup 1.0000x reference)
import sys
import numpy as np
import ml_dtypes

try:
    import concourse.bass as bass
except Exception:
    sys.path.insert(0, "/opt/trn_rl_repo")
    import concourse.bass as bass
import concourse.mybir as mybir
from concourse.bass_utils import run_bass_kernel_spmd

F32 = mybir.dt.float32
BF16 = mybir.dt.bfloat16
AF = mybir.ActivationFunctionType
ALU = mybir.AluOpType
AX = mybir.AxisListType
BFd = ml_dtypes.bfloat16

D = 1024
S = 2048
NT = 16
NC = 8
NEXP = 8

PARTS = (1, 2, 3, 4)

STREAMS = ("pe", "dve", "act", "pool", "sd", "gd")
SECTION = {"pe": "tensor", "dve": "vector", "act": "scalar", "pool": "gpsimd",
           "sd": "sync", "gd": "gpsimd"}
INC = {"pe": 1, "dve": 1, "act": 1, "pool": 1, "sd": 16, "gd": 16}
SAME_ENG_SAFE = 14


class Prog:
    def __init__(self):
        self.items = {"tensor": [], "vector": [], "scalar": [], "gpsimd": [], "sync": []}
        self.cnt = {s: 0 for s in STREAMS}
        self.lastw = {}
        self.lastr = {}
        self.sat = {s: {} for s in STREAMS}

    def emit(self, st, fn, r=(), w=()):
        deps = {}
        for t in r:
            lw = self.lastw.get(t)
            if lw is not None and deps.get(lw[0], -1) < lw[1]:
                deps[lw[0]] = lw[1]
        for t in w:
            lw = self.lastw.get(t)
            if lw is not None and deps.get(lw[0], -1) < lw[1]:
                deps[lw[0]] = lw[1]
            for s2, q in self.lastr.get(t, {}).items():
                if deps.get(s2, -1) < q:
                    deps[s2] = q
        self.cnt[st] += 1
        seq = self.cnt[st]
        waits = []
        if st in ("sd", "gd") and seq > 1:
            if self.sat[st].get(st, 0) < seq - 1:
                self.sat[st][st] = seq - 1
                waits.append((st, seq - 1))
        for s2, q in deps.items():
            if s2 == st:
                if st in ("dve", "act", "pool") and (seq - 1) - q < SAME_ENG_SAFE:
                    if self.sat[st].get(st, 0) < q:
                        self.sat[st][st] = q
                        waits.append((st, q))
                continue
            if self.sat[st].get(s2, 0) >= q:
                continue
            self.sat[st][s2] = q
            waits.append((s2, q))
        self.items[SECTION[st]].append((st, waits, fn))
        for t in w:
            self.lastw[t] = (st, seq)
            self.lastr[t] = {}
        for t in r:
            d = self.lastr.setdefault(t, {})
            if d.get(st, 0) < seq:
                d[st] = seq
        return seq


def tgr(name, lo, hi, blk=512):
    return [name + str(i) for i in range(lo // blk, (hi + blk - 1) // blk)]


def _repack(inputs):
    f = np.ascontiguousarray
    d = {}
    x = np.asarray(inputs["L0_attn_a_x"], np.float32)
    d["x"] = [f(x[b].reshape(NT, 128, D).transpose(1, 0, 2).reshape(128, NT * D))
              for b in range(8)]

    def colpack(Wm, kb=128):
        K, M = Wm.shape
        nkc = K // kb
        npair = M // 128
        return f(Wm.reshape(nkc, kb, npair, 128).transpose(2, 1, 0, 3)
                 .reshape(npair * kb, nkc * 128))

    d["wq_a"] = colpack(np.asarray(inputs["L0_attn_a_Wq"], np.float32))
    d["wk_a"] = colpack(np.asarray(inputs["L0_attn_a_Wk"], np.float32))
    d["wv_a"] = colpack(np.asarray(inputs["L0_attn_a_Wv"], np.float32))
    d["wo_a"] = f(np.asarray(inputs["L0_attn_a_Wo"], np.float32))
    W1 = np.asarray(inputs["L0_attn_a_W1"], np.float32)
    d["w1_a"] = f(W1.reshape(8, 128, 16, 128).transpose(2, 1, 0, 3).reshape(2048, 1024))
    d["w2_a"] = f(np.asarray(inputs["L0_attn_a_W2"], np.float32))

    d["wq_f"] = colpack(np.asarray(inputs["L0_ffn_Wq"], np.float32))
    Wdkv = np.asarray(inputs["L0_ffn_Wdkv"], np.float32)
    d["wdkv"] = f(Wdkv.reshape(8, 128, 4, 128).transpose(2, 1, 0, 3).reshape(512, 1024))
    Wuk = np.asarray(inputs["L0_ffn_Wuk"], np.float32)
    d["wuk"] = f(Wuk.reshape(4, 128, 8, 128).transpose(2, 1, 0, 3).reshape(1024, 512))
    Wuv = np.asarray(inputs["L0_ffn_Wuv"], np.float32)
    d["wuv"] = f(Wuv.reshape(4, 128, 8, 128).transpose(2, 1, 0, 3).reshape(1024, 512))
    d["wo_f"] = f(np.asarray(inputs["L0_ffn_Wo"], np.float32))

    wqkv = np.asarray(inputs["L0_attn_b_w_qkv"], np.float32)
    d["wq_b"] = colpack(wqkv[:, 0:1024])
    d["wk_b"] = colpack(wqkv[:, 1024:2048])
    d["wv_b"] = colpack(wqkv[:, 2048:3072])

    w1m = np.asarray(inputs["L0_ffn_w1"], np.float32)
    d["w1_m"] = f(w1m.reshape(NEXP, 8, 128, 16, 128).transpose(0, 3, 2, 1, 4)
                  .reshape(NEXP * 2048, 1024))
    d["w2_m"] = f(np.asarray(inputs["L0_ffn_w2"], np.float32).reshape(NEXP * 2048, D))

    d["gws"] = f(np.asarray(inputs["L0_ffn_gate_w"], np.float32)
                 .reshape(8, 128, 8).transpose(1, 0, 2).reshape(128, 64))

    def pcol(v):
        return np.asarray(v, np.float32).reshape(8, 128).T

    sm = np.zeros((128, 96), np.float32)
    sm[:, 0:8] = pcol(inputs["L0_attn_a_lnq_s"]); sm[:, 8:16] = pcol(inputs["L0_attn_a_lnq_b"])
    sm[:, 16:24] = pcol(inputs["L0_attn_a_lnk_s"]); sm[:, 24:32] = pcol(inputs["L0_attn_a_lnk_b"])
    sm[:, 32:40] = pcol(inputs["L0_attn_a_lnv_s"]); sm[:, 40:48] = pcol(inputs["L0_attn_a_lnv_b"])
    sm[:, 48:56] = pcol(inputs["L0_attn_a_bq"]); sm[:, 56:64] = pcol(inputs["L0_attn_a_bk"])
    sm[:, 64:72] = pcol(inputs["L0_attn_a_bv"])
    sm[:, 72:88] = np.asarray(inputs["L0_attn_a_b1"], np.float32).reshape(16, 128).T
    d["smalls"] = f(sm)

    rows = np.zeros((8, 1024), np.float32)
    rows[0] = np.asarray(inputs["L0_attn_a_lnm_s"], np.float32)
    rows[1] = np.asarray(inputs["L0_attn_a_lnm_b"], np.float32)
    rows[2] = np.asarray(inputs["L0_ffn_g1"], np.float32)
    rows[3] = np.asarray(inputs["L0_ffn_g2"], np.float32)
    rows[4] = np.asarray(inputs["L0_attn_a_bo"], np.float32)
    rows[5] = np.asarray(inputs["L0_attn_a_b2"], np.float32)
    d["rows"] = f(rows)

    d["idn"] = np.eye(128, dtype=np.float32).astype(BFd)
    mk = np.zeros((128, 4, 512), np.float32)
    ii = np.arange(128)[:, None]
    jj = np.arange(512)[None, :]
    for rr in range(4):
        mk[:, rr, :] = np.where(jj <= rr * 128 + ii, 0.0, -30000.0)
    d["maskc"] = f(mk.reshape(128, 2048)).astype(BFd)
    return d


def build_program():
    nc = bass.Bass("TRN2", target_bir_lowering=False)
    dr = {}
    dr["x"] = nc.dram_tensor("x", [128, NT * D], F32, kind="ExternalInput").ap()
    for n, shape in [("wq_a", [1024, 1024]), ("wk_a", [1024, 1024]), ("wv_a", [1024, 1024]),
                     ("wo_a", [1024, 1024]), ("w1_a", [2048, 1024]), ("w2_a", [2048, 1024]),
                     ("wq_f", [1024, 1024]), ("wdkv", [512, 1024]), ("wuk", [1024, 512]),
                     ("wuv", [1024, 512]), ("wo_f", [1024, 1024]),
                     ("wq_b", [1024, 1024]), ("wk_b", [1024, 1024]), ("wv_b", [1024, 1024]),
                     ("w1_m", [NEXP * 2048, 1024]), ("w2_m", [NEXP * 2048, 1024]),
                     ("gws", [128, 64]), ("smalls", [128, 96]), ("rows", [8, 1024])]:
        dr[n] = nc.dram_tensor(n, shape, F32, kind="ExternalInput").ap()
    dr["idn"] = nc.dram_tensor("idn", [128, 128], BF16, kind="ExternalInput").ap()
    dr["maskc"] = nc.dram_tensor("maskc", [128, 2048], BF16, kind="ExternalInput").ap()
    out = nc.dram_tensor("out", [128, NT * D], F32, kind="ExternalOutput").ap()

    from contextlib import ExitStack
    es = ExitStack()
    X = es.enter_context(nc.sbuf_tensor("X", [128, 16384], BF16))
    TT = es.enter_context(nc.sbuf_tensor("TT", [128, 16384], BF16))
    ATb = es.enter_context(nc.sbuf_tensor("ATb", [128, 16384], BF16))
    AR = es.enter_context(nc.sbuf_tensor("AR", [128, 40960], BF16))
    IDN = es.enter_context(nc.sbuf_tensor("IDN", [128, 128], BF16))
    MKC = es.enter_context(nc.sbuf_tensor("MKC", [128, 2048], BF16))
    SM = es.enter_context(nc.sbuf_tensor("SM", [128, 96], F32))
    SMB = es.enter_context(nc.sbuf_tensor("SMB", [128, 96], BF16))
    GWS = es.enter_context(nc.sbuf_tensor("GWS", [128, 64], BF16))
    ST = es.enter_context(nc.sbuf_tensor("ST", [128, 64], F32))
    STA = es.enter_context(nc.sbuf_tensor("STA", [128, 40], F32))
    ONES1 = es.enter_context(nc.sbuf_tensor("ONES1", [1, 128], BF16))
    EPS = es.enter_context(nc.sbuf_tensor("EPS", [128, 2], F32))
    BOROW = es.enter_context(nc.sbuf_tensor("BOROW", [1, 1024], BF16))
    B2ROW = es.enter_context(nc.sbuf_tensor("B2ROW", [1, 1024], BF16))
    RSTG = es.enter_context(nc.sbuf_tensor("RSTG", [1, 1024], BF16))
    STGE = es.enter_context(nc.sbuf_tensor("STGE", [1, 2048], BF16))
    BVT = es.enter_context(nc.sbuf_tensor("BVT", [1, 128], BF16))
    BVC = es.enter_context(nc.sbuf_tensor("BVC", [128, 1], BF16))
    CT8 = es.enter_context(nc.sbuf_tensor("CT8", [8, 2048], BF16))
    CMB = es.enter_context(nc.sbuf_tensor("CMB", [128, 128], BF16))
    PS_SC = es.enter_context(nc.psum_tensor("PS_SC", [128, 2048], F32))
    PS_TR = es.enter_context(nc.psum_tensor("PS_TR", [128, 1024], F32))
    PS_A = es.enter_context(nc.psum_tensor("PS_A", [128, 512], F32))
    PS_B = es.enter_context(nc.psum_tensor("PS_B", [128, 512], F32))
    sems = {s: es.enter_context(nc.semaphore("sem_" + s)) for s in STREAMS}

    p = Prog()

    def AB(off, n):
        return AR[:, off:off + n]

    def AF32(offb, nf):
        return AR[:].bitcast(F32)[:, offb // 2: offb // 2 + nf]

    def atag(off_bf, n_bf):
        return tgr("AR", off_bf, off_bf + n_bf)

    def attag(off, n):
        return tgr("AT", off, off + n, 1024)

    def xtg(off, n):
        return tgr("X", off, off + n)

    def ttg(kc, tlo, thi):
        return ["TT%d_%d" % (kc, t) for t in range(tlo, thi)]

    def ptrtag(bflo, bfhi):
        return tgr("PTRB", bflo, bfhi, 1024)

    QOF = [0, 8192]
    VOF = [16384, 18432]
    PSOF = [20480, 22528]
    PTOF = [24576, 25088]
    WSOF = [25728, 28800]
    CTOF = 32768
    WOOF = 32768
    W2OF = 0
    W1OF = 16384
    SILOF = 32768
    HUOF = 0
    NRMC_A = 0
    NRMY_A = 8192
    NRMC_F = 24576
    NRMY_F = 20480

    def dma(st, o, i, r=(), w=()):
        p.emit(st, (lambda o=o, i=i: (lambda e: e.dma_start(out=o, in_=i)))(), r=r, w=w)

    def mm(psum, lhsT, rhs, start, stop, tp=None, r=(), w=()):
        p.emit("pe", (lambda psum=psum, lhsT=lhsT, rhs=rhs, start=start, stop=stop, tp=tp:
                      (lambda e: e.matmul(psum, lhsT, rhs, start=start, stop=stop,
                                          tile_position=tp)))(), r=r, w=w)

    def trans(psum, inp, r=(), w=()):
        p.emit("pe", (lambda psum=psum, inp=inp:
                      (lambda e: e.transpose(psum, inp, IDN[:])))(), r=r, w=w)

    def act(o, i, func, bias=None, scale=1.0, accum=None, r=(), w=()):
        kw = {}
        if bias is not None:
            kw["bias"] = bias
        if accum is not None:
            kw["accum_out"] = accum
        p.emit("act", (lambda o=o, i=i, func=func, scale=scale, kw=kw:
                       (lambda e: e.activation(o, i, func, scale=scale, **kw)))(), r=r, w=w)

    def vop(st, meth, *args, r=(), w=()):
        p.emit(st, (lambda meth=meth, args=args:
                    (lambda e: getattr(e, meth)(*args)))(), r=r, w=w)

    TRV = PS_TR[:].bitcast(BF16)

    dma("gd", X[:], dr["x"], w=xtg(0, 16384))
    dma("sd", IDN[:], dr["idn"], w=["IDN"])
    dma("sd", MKC[:], dr["maskc"], w=["MKC"])
    dma("sd", SM[:], dr["smalls"], w=["SM"])
    dma("gd", GWS[:], dr["gws"], w=["GWS"])
    dma("gd", BOROW[:], dr["rows"][4:5, :], w=["BOROW"])
    dma("gd", B2ROW[:], dr["rows"][5:6, :], w=["B2ROW"])
    vop("pool", "memset", ONES1[:], 1.0, w=["ONES1"])
    vop("pool", "tensor_copy", SMB[:], SM[:], r=["SM"], w=["SMB"])
    vop("pool", "memset", EPS[:, 0:1], 1e-5, w=["EPS"])
    vop("pool", "memset", EPS[:, 1:2], 1e-6, w=["EPS"])

    def ln_tile(src, stags, Cof, rot, eps, stn):
        C = AF32(Cof, 1024)
        ctags = atag(Cof, 2048)
        C2of = Cof + 4096
        C2 = AF32(C2of, 1024)
        c2tags = atag(C2of, 2048)
        mu = ST[:, 0 + rot:1 + rot] if stn == "STn" else ST[:, 30 + rot:31 + rot]
        vv = ST[:, 2 + rot:3 + rot] if stn == "STn" else ST[:, 32 + rot:33 + rot]
        rs = ST[:, 4 + rot:5 + rot] if stn == "STn" else ST[:, 34 + rot:35 + rot]
        t = [stn + str(rot)]
        vop("dve", "reduce_sum", mu, src, AX.X, r=stags, w=t)
        vop("dve", "tensor_single_scalar", mu, mu, 1.0 / D, ALU.mult, r=t, w=t)
        vop("dve", "scalar_tensor_tensor", C, src, mu, src, ALU.subtract, ALU.bypass,
            r=stags + t, w=ctags)
        act(C2, C, AF.Square, accum=vv, r=ctags + t, w=c2tags + t)
        vop("dve", "tensor_single_scalar", vv, vv, 1.0 / D, ALU.mult, r=t, w=t)
        act(rs, vv, AF.Sqrt, bias=EPS[:, 0:1] if eps == 1e-5 else EPS[:, 1:2],
            r=t + ["EPS"], w=t)
        vop("dve", "reciprocal", rs, rs, r=t, w=t)
        return C, ctags, rs, t

    def rms_tile(src, stags, Cof, rot, eps, stn="STn"):
        C2of = Cof + 4096
        C2 = AF32(C2of, 1024)
        vv = ST[:, 2 + rot:3 + rot]
        rs = ST[:, 4 + rot:5 + rot]
        t = [stn + str(rot)]
        act(C2, src, AF.Square, accum=vv, r=stags + t, w=atag(C2of, 2048) + t)
        vop("dve", "tensor_single_scalar", vv, vv, 1.0 / D, ALU.mult, r=t, w=t)
        act(rs, vv, AF.Sqrt, bias=EPS[:, 0:1] if eps == 1e-5 else EPS[:, 1:2],
            r=t + ["EPS"], w=t)
        vop("dve", "reciprocal", rs, rs, r=t, w=t)
        return rs, t

    def transpose_to_TT(YH, ytags, tau, dst=None):
        dstt = TT if dst is None else dst
        t3 = dstt[:].rearrange("p (c m) -> p c m", c=8)
        for grp in range(2):
            bank = (tau * 2 + grp) % 2
            for q in range(4):
                c = grp * 4 + q
                trw = TRV[:, bank * 1024 + q * 128: bank * 1024 + (q + 1) * 128]
                trans(trw, YH[:, c * 128:(c + 1) * 128], r=ytags + ["IDN"],
                      w=ptrtag(bank * 1024, bank * 1024 + 1024))
            vop("dve", "tensor_copy",
                t3[:, grp * 4:grp * 4 + 4, tau * 128: tau * 128 + 128],
                TRV[:, bank * 1024: bank * 1024 + 512].rearrange("p (c m) -> p c m", c=4),
                r=ptrtag(bank * 1024, bank * 1024 + 1024),
                w=["TT%d_%d" % (grp * 4 + q, tau) for q in range(4)])

    def norm_to_TT(kind, src_fn, stags_fn, cbase, ybase, gbc=None, gbc_tags=(), eps=1e-5):
        for tau in range(NT):
            rot = tau % 2
            Cof = cbase + rot * 2048
            yof = ybase + rot * 2048
            YH = AB(yof, 1024)
            ytags = atag(yof, 1024)
            src = src_fn(tau)
            stags = stags_fn(tau)
            if kind == "ln":
                C, ctags, rs, t = ln_tile(src, stags, Cof, rot, eps, "STn")
                vop("dve", "tensor_single_scalar", YH, C, rs, ALU.mult, r=ctags + t, w=ytags)
            elif kind == "ln2":
                C, ctags, rs, t = ln_tile(src, stags, Cof, rot, eps, "STn")
                vop("dve", "tensor_single_scalar", C, C, rs, ALU.mult, r=ctags + t, w=ctags)
                C, ctags, rs, t = ln_tile(C, ctags, Cof, rot, eps, "STn")
                vop("dve", "tensor_single_scalar", YH, C, rs, ALU.mult, r=ctags + t, w=ytags)
            elif kind == "lnaff":
                C, ctags, rs, t = ln_tile(src, stags, Cof, rot, eps, "STn")
                vop("dve", "tensor_single_scalar", C, C, rs, ALU.mult, r=ctags + t, w=ctags)
                vop("pool", "tensor_tensor", C, C, gbc[0], ALU.mult,
                    r=ctags + list(gbc_tags), w=ctags)
                vop("pool", "tensor_tensor", YH, C, gbc[1], ALU.add,
                    r=ctags + list(gbc_tags), w=ytags)
            elif kind == "lnrms":
                C, ctags, rs, t = ln_tile(src, stags, Cof, rot, 1e-5, "STn")
                vop("dve", "tensor_single_scalar", C, C, rs, ALU.mult, r=ctags + t, w=ctags)
                rs2, t2 = rms_tile(C, ctags, Cof, rot, 1e-6)
                vop("dve", "tensor_single_scalar", C, C, rs2, ALU.mult, r=ctags + t2, w=ctags)
                vop("pool", "tensor_tensor", YH, C, gbc[0], ALU.mult,
                    r=ctags + list(gbc_tags), w=ytags)
            elif kind == "rmsg":
                rs2, t2 = rms_tile(src, stags, Cof, rot, 1e-6)
                vop("dve", "tensor_single_scalar", AF32(Cof, 1024), src, rs2, ALU.mult,
                    r=stags + t2, w=atag(Cof, 2048))
                vop("pool", "tensor_tensor", YH, AF32(Cof, 1024), gbc[0], ALU.mult,
                    r=atag(Cof, 2048) + list(gbc_tags), w=ytags)
            transpose_to_TT(YH, ytags, tau)

    def build_bcast(row_idx, dstof):
        dma("gd", RSTG[0:1, :], dr["rows"][row_idx:row_idx + 1, :], r=["RSTG"], w=["RSTG"])
        for q in range(2):
            mm(PS_TR[:, q * 512:(q + 1) * 512], ONES1[:],
               RSTG[0:1, q * 512:(q + 1) * 512], True, True,
               r=["RSTG", "ONES1"], w=ptrtag(q * 1024, q * 1024 + 1024))
        vop("dve", "tensor_copy", AB(dstof, 1024), PS_TR[:, 0:1024],
            r=ptrtag(0, 2048), w=atag(dstof, 1024))

    def qkv_pair(c, sset, wq_dr, wk_dr, wv_dr, nkc_q, nkc_kv, srcq, srcq_t, srckv, srckv_t,
                 qk_beta=False, qscale=None):
        wof = WSOF[sset]
        qslot, kslot, vslot = wof, wof + 1024, wof + 2048
        tq = atag(qslot, 1024)
        tk = atag(kslot, 1024)
        tv = atag(vslot, 1024)
        dma("gd", AB(qslot, nkc_q * 128), wq_dr[c * 128:(c + 1) * 128, :], w=tq)
        dma("gd", AB(kslot, nkc_kv * 128), wk_dr[c * 128:(c + 1) * 128, :], w=tk)
        dma("gd", AB(vslot, nkc_kv * 128), wv_dr[c * 128:(c + 1) * 128, :], w=tv)
        rot = c % 2
        bq = ST[:, 24 + rot:25 + rot]
        bk = ST[:, 26 + rot:27 + rot]
        bv = ST[:, 28 + rot:29 + rot]
        stb = ["STb%d" % rot]
        if qk_beta:
            for (slot, stags, lns, lnb, bcol, beta, extra) in (
                    (qslot, tq, 0, 8, 48, bq, 1.0 / 32.0),
                    (kslot, tk, 16, 24, 56, bk, None),
                    (vslot, tv, 32, 40, 64, bv, None)):
                for kc in range(8):
                    mm(PS_A[:, 64:65], AB(slot, 1024)[:, kc * 128:(kc + 1) * 128],
                       SMB[:, lnb + kc:lnb + kc + 1], start=(kc == 0), stop=(kc == 7),
                       r=stags + ["SMB"], w=["PSA"])
                vop("dve", "tensor_tensor", beta, PS_A[:, 64:65],
                    SM[:, bcol + c:bcol + c + 1], ALU.add, r=["PSA", "SM"], w=stb)
                if extra is not None:
                    vop("dve", "tensor_single_scalar", beta, beta, extra, ALU.mult,
                        r=stb, w=stb)
                for kc in range(8):
                    sc = SM[:, lns + kc:lns + kc + 1]
                    vop("dve", "tensor_single_scalar",
                        AB(slot, 1024)[:, kc * 128:(kc + 1) * 128],
                        AB(slot, 1024)[:, kc * 128:(kc + 1) * 128], sc, ALU.mult,
                        r=stags + ["SM"], w=stags)
                    if extra is not None:
                        vop("dve", "tensor_single_scalar",
                            AB(slot, 1024)[:, kc * 128:(kc + 1) * 128],
                            AB(slot, 1024)[:, kc * 128:(kc + 1) * 128], extra, ALU.mult,
                            r=stags, w=stags)
        elif qscale is not None:
            vop("dve", "tensor_single_scalar", AB(qslot, nkc_q * 128),
                AB(qslot, nkc_q * 128), qscale, ALU.mult, r=tq, w=tq)
        qof = QOF[sset]
        for (slot, stags, dof, nkc, srcf, srcft, beta) in (
                (qslot, tq, qof, nkc_q, srcq, srcq_t, bq if qk_beta else None),
                (kslot, tk, qof + 4096, nkc_kv, srckv, srckv_t, bk if qk_beta else None)):
            for tc in range(4):
                for kc in range(nkc):
                    mm(PS_SC[:, tc * 512:(tc + 1) * 512],
                       AB(slot, nkc * 128)[:, kc * 128:(kc + 1) * 128],
                       srcf(kc, tc), start=(kc == 0), stop=(kc == nkc - 1),
                       r=stags + srcft(kc, tc), w=tgr("PSSC", tc * 512, tc * 512 + 512))
            scr = tgr("PSSC", 0, 2048)
            tA = atag(dof, 2048)
            tB = atag(dof + 2048, 2048)
            if beta is not None:
                vop("dve", "tensor_single_scalar", AB(dof, 2048)[0:64, :], PS_SC[0:64, :],
                    beta[0:64, :], ALU.add, r=scr + stb, w=tA)
                vop("dve", "tensor_single_scalar", AB(dof + 2048, 2048)[64:128, :],
                    PS_SC[64:128, :], beta[64:128, :], ALU.add, r=scr + stb, w=tB)
            else:
                vop("dve", "tensor_copy", AB(dof, 2048)[0:64, :], PS_SC[0:64, :],
                    r=scr, w=tA)
                vop("dve", "tensor_copy", AB(dof + 2048, 2048)[64:128, :],
                    PS_SC[64:128, :], r=scr, w=tB)
            dma("sd", AB(dof, 2048)[64:128, :], AB(dof, 2048)[0:64, :], r=tA, w=tA)
            dma("sd", AB(dof + 2048, 2048)[0:64, :], AB(dof + 2048, 2048)[64:128, :],
                r=tB, w=tB)
        vof = VOF[sset]
        if qk_beta:
            for kc in range(8):
                mm(PS_A[0:1, 128:256], SMB[:, 40 + kc:41 + kc],
                   AB(vslot, 1024)[:, kc * 128:(kc + 1) * 128],
                   start=(kc == 0), stop=(kc == 7), r=tv + ["SMB"], w=["PSA"])
            vop("dve", "tensor_copy", BVC[:], SM[:, 64 + c:65 + c], r=["SM"], w=["BVC"])
            trans(PS_A[:].bitcast(BF16)[0:1, 512:640], BVC[:], r=["BVC", "IDN"],
                  w=["PSA"])
            vop("dve", "tensor_copy", BVT[0:1, :], PS_A[:].bitcast(BF16)[0:1, 512:640],
                r=["PSA"], w=["BVT"])
            vop("dve", "tensor_tensor", BVT[0:1, :], PS_A[0:1, 128:256], BVT[0:1, :],
                ALU.add, r=["PSA", "BVT"], w=["BVT"])
        for tt in range(NT):
            for kc in range(nkc_kv):
                mm(PS_SC[:, tt * 128:(tt + 1) * 128], srckv(kc, None, tt),
                   AB(vslot, nkc_kv * 128)[:, kc * 128:(kc + 1) * 128],
                   start=(kc == 0), stop=(not qk_beta and kc == nkc_kv - 1),
                   r=tv + srckv_t(kc, None, tt),
                   w=tgr("PSSC", tt * 128, tt * 128 + 128))
            if qk_beta:
                mm(PS_SC[:, tt * 128:(tt + 1) * 128], ONES1[:], BVT[0:1, :],
                   start=False, stop=True, r=["ONES1", "BVT"],
                   w=tgr("PSSC", tt * 128, tt * 128 + 128))
        vop("dve", "tensor_copy", AB(vof, 2048), PS_SC[:], r=tgr("PSSC", 0, 2048),
            w=atag(vof, 2048))

    def attention_step(qof, kof, vof, hh, it, njc, diag_r, mask1, psel, evac_fn):
        qtag = atag(qof, 2048)
        ktag = atag(kof, 2048)
        vtag = atag(vof, 2048)
        scw = tgr("PSSC", 0, njc * 512)
        jc = 0
        while jc < njc:
            if jc + 1 < njc:
                mm(PS_SC[:, jc * 512:(jc + 1) * 512],
                   AB(qof, 2048)[0:64, it * 128:(it + 1) * 128],
                   AB(kof, 2048)[0:64, jc * 512:(jc + 1) * 512], True, True, tp=(0, 0),
                   r=qtag + ktag, w=tgr("PSSC", jc * 512, jc * 512 + 512))
                mm(PS_SC[:, (jc + 1) * 512:(jc + 2) * 512],
                   AB(qof, 2048)[64:128, it * 128:(it + 1) * 128],
                   AB(kof, 2048)[64:128, (jc + 1) * 512:(jc + 2) * 512], True, True,
                   tp=(64, 0), r=qtag + ktag,
                   w=tgr("PSSC", (jc + 1) * 512, (jc + 1) * 512 + 512))
                jc += 2
            else:
                mm(PS_SC[:, jc * 512:(jc + 1) * 512],
                   AB(qof, 2048)[0:64, it * 128:(it + 1) * 128],
                   AB(kof, 2048)[0:64, jc * 512:(jc + 1) * 512], True, True, tp=(0, 0),
                   r=qtag + ktag, w=tgr("PSSC", jc * 512, jc * 512 + 512))
                jc += 1
        if diag_r is not None:
            dc = tgr("PSSC", (njc - 1) * 512, njc * 512)
            vop("dve", "tensor_tensor", PS_SC[:, (njc - 1) * 512: njc * 512],
                PS_SC[:, (njc - 1) * 512: njc * 512],
                MKC[:, diag_r * 512:(diag_r + 1) * 512], ALU.add, r=dc + ["MKC"], w=dc)
        if mask1:
            dc = tgr("PSSC", 1536, 2048)
            vop("dve", "tensor_single_scalar", PS_SC[0:1, 2047:2048],
                PS_SC[0:1, 2047:2048], -999.0, ALU.add, r=dc, w=dc)
        rot = psel % 2
        sta = ["STa%d" % rot]
        nmb = 4 + rot * 4
        nm4 = STA[:, nmb:nmb + njc]
        accb = 12 + rot * 4
        acc4 = STA[:, accb:accb + njc]
        gq = STA[:, 20 + rot:21 + rot]
        f4 = STA[:, 24 + rot * 4:24 + rot * 4 + njc]
        ssum = STA[:, 32 + rot:33 + rot]
        rr = STA[:, 34 + rot:35 + rot]
        psof = PSOF[rot]
        ptags = atag(psof, njc * 512)
        for q in range(njc):
            vop("dve", "reduce_max", STA[:, nmb + q:nmb + q + 1],
                PS_SC[:, q * 512:(q + 1) * 512], AX.X, None, None, True,
                r=tgr("PSSC", q * 512, q * 512 + 512), w=sta)
            act(AB(psof + q * 512, 512), PS_SC[:, q * 512:(q + 1) * 512], AF.Exp,
                bias=STA[:, nmb + q:nmb + q + 1],
                accum=STA[:, accb + q:accb + q + 1],
                r=tgr("PSSC", q * 512, q * 512 + 512) + sta,
                w=atag(psof + q * 512, 512) + sta)
        if njc >= 2:
            vop("dve", "tensor_reduce", gq, nm4, AX.X, ALU.min, r=sta, w=sta)
            vop("dve", "tensor_single_scalar", f4, nm4, gq, ALU.subtract, r=sta, w=sta)
            act(f4, f4, AF.Exp, scale=-1.0, r=sta, w=sta)
            for q in range(njc):
                vop("pool", "tensor_single_scalar", AB(psof + q * 512, 512),
                    AB(psof + q * 512, 512), STA[:, 24 + rot * 4 + q:24 + rot * 4 + q + 1],
                    ALU.mult, r=atag(psof + q * 512, 512) + sta,
                    w=atag(psof + q * 512, 512))
            vop("dve", "tensor_tensor", acc4, acc4, f4, ALU.mult, r=sta, w=sta)
            vop("dve", "reduce_sum", ssum, acc4, AX.X, r=sta, w=sta)
        else:
            vop("dve", "tensor_copy", ssum, acc4, r=sta, w=sta)
        vop("dve", "reciprocal", rr, ssum, r=sta, w=sta)
        njt = it + 1 if diag_r is not None else NT
        psum_att = (PS_A if rot == 0 else PS_B)[:, 0:64]
        attw = ["PSA" if rot == 0 else "PSB"]
        for g0 in range(0, njt, 4):
            n = min(4, njt - g0)
            gs = (g0 // 4) % 2
            for q in range(n):
                jt = g0 + q
                trans(TRV[:, gs * 1024 + q * 128: gs * 1024 + q * 128 + 128],
                      AB(psof, njc * 512)[:, jt * 128:(jt + 1) * 128],
                      r=ptags + ["IDN"],
                      w=ptrtag(gs * 1024, gs * 1024 + 1024))
            ptof = PTOF[gs]
            vop("dve", "tensor_copy", AB(ptof, 512)[:, 0:n * 128],
                TRV[:, gs * 1024:gs * 1024 + n * 128],
                r=ptrtag(gs * 1024, gs * 1024 + 1024), w=atag(ptof, 512))
            for q in range(n):
                jt = g0 + q
                mm(psum_att, AB(ptof, 512)[:, q * 128:(q + 1) * 128],
                   AB(vof, 2048)[:, jt * 128 + hh * 64: jt * 128 + hh * 64 + 64],
                   start=(jt == 0), stop=(jt == njt - 1),
                   r=atag(ptof, 512) + vtag, w=attw)
        evac_fn(psum_att, rr, attw, sta)

    def wo_phase(wo_dr, with_bo, write_hu):
        dma("gd", AB(WOOF, 8192).rearrange("p (kc n) -> p kc n", kc=8),
            wo_dr.rearrange("(kc p) n -> p kc n", p=128), w=atag(WOOF, 8192))
        wo_t = atag(WOOF, 8192)
        for tt in range(NT):
            rot = tt % 2
            yof = PSOF[0] + rot * 2048
            Y1 = AF32(yof, 1024)
            ytags = atag(yof, 1024 * 2)
            C, ctags, rs, t = ln_tile(X[:, tt * 1024:(tt + 1) * 1024],
                                      xtg(tt * 1024, tt * 1024 + 1024),
                                      yof, rot, 1e-5, "STw")
            vop("dve", "tensor_single_scalar", Y1, C, rs, ALU.mult, r=ctags + t, w=ytags)
            for oc in range(2):
                psq = PS_SC[:, (tt % 2) * 1024 + oc * 512: (tt % 2) * 1024 + oc * 512 + 512]
                pst = tgr("PSSC", (tt % 2) * 1024 + oc * 512,
                          (tt % 2) * 1024 + oc * 512 + 512)
                for kc in range(8):
                    mm(psq, ATb[:, kc * 2048 + tt * 128: kc * 2048 + tt * 128 + 128],
                       AB(WOOF, 8192)[:, kc * 1024 + oc * 512: kc * 1024 + oc * 512 + 512],
                       start=(kc == 0), stop=(with_bo is None and kc == 7),
                       r=attag(kc * 2048 + tt * 128, 128) + wo_t, w=pst)
                if with_bo is not None:
                    mm(psq, ONES1[:], with_bo[0:1, oc * 512:(oc + 1) * 512],
                       start=False, stop=True, r=["ONES1", "BOROW"], w=pst)
                hu = AB(HUOF, 16384)[:, tt * 1024 + oc * 512: tt * 1024 + oc * 512 + 512]
                hut = atag(tt * 1024 + oc * 512, 512)
                vop("dve", "tensor_tensor", hu, psq, Y1[:, oc * 512:(oc + 1) * 512],
                    ALU.add, r=pst + ytags, w=hut)
                xr = xtg(tt * 1024 + oc * 512, tt * 1024 + oc * 512 + 512)
                vop("pool", "tensor_tensor",
                    X[:, tt * 1024 + oc * 512: tt * 1024 + oc * 512 + 512],
                    X[:, tt * 1024 + oc * 512: tt * 1024 + oc * 512 + 512], hu, ALU.add,
                    r=hut + xr, w=xr)

    def mlp_halves(w1_slice, w2_slice, gelu, b1, b2row, cbc_src):
        dma("gd", AB(W1OF, 16384).rearrange("p (hc k) -> p hc k", hc=16),
            w1_slice.rearrange("(hc p) k -> p hc k", p=128), w=atag(W1OF, 16384))
        dma("gd", AB(W2OF, 16384).rearrange("p (hc n) -> p hc n", hc=16),
            w2_slice.rearrange("(hc p) n -> p hc n", p=128), w=atag(W2OF, 16384))
        w1t = atag(W1OF, 16384)
        w2t = atag(W2OF, 16384)
        for half in range(2):
            if cbc_src is not None:
                for q in range(2):
                    mm(PS_TR[:, q * 512:(q + 1) * 512], ONES1[:],
                       cbc_src[0:1, half * 1024 + q * 512: half * 1024 + (q + 1) * 512],
                       True, True, r=["STGE", "ONES1"],
                       w=ptrtag(q * 1024, q * 1024 + 1024))
                vop("dve", "tensor_copy", AB(SILOF + 4096, 1024), PS_TR[:, 0:1024],
                    r=ptrtag(0, 2048), w=atag(SILOF + 4096, 1024))
            for hc in range(16):
                psr = PS_SC[:, (hc % 2) * 1024:(hc % 2) * 1024 + 1024]
                pst = tgr("PSSC", (hc % 2) * 1024, (hc % 2) * 1024 + 1024)
                for tc2 in range(2):
                    tc = half * 2 + tc2
                    for kc in range(8):
                        mm(psr[:, tc2 * 512:(tc2 + 1) * 512],
                           AB(W1OF, 16384)[:, hc * 1024 + kc * 128: hc * 1024 + kc * 128 + 128],
                           TT[:, kc * 2048 + tc * 512: kc * 2048 + (tc + 1) * 512],
                           start=(kc == 0), stop=(kc == 7),
                           r=w1t + ttg(kc, tc * 4, tc * 4 + 4), w=pst)
                fn = AF.Gelu_apprx_tanh if gelu else AF.Silu
                bias = SM[:, 72 + hc:73 + hc] if b1 else None
                if cbc_src is None:
                    act(ATb[:, hc * 1024:(hc + 1) * 1024], psr, fn, bias=bias,
                        r=pst + (["SM"] if b1 else []), w=attag(hc * 1024, 1024))
                else:
                    silof = SILOF + 2048 + (hc % 2) * 1024
                    act(AB(silof, 1024), psr, fn, r=pst, w=atag(silof, 1024))
                    vop("pool", "tensor_tensor", ATb[:, hc * 1024:(hc + 1) * 1024],
                        AB(silof, 1024), AB(SILOF + 4096, 1024), ALU.mult,
                        r=atag(silof, 1024) + atag(SILOF + 4096, 1024),
                        w=attag(hc * 1024, 1024))
            for tl in range(8):
                tt = half * 8 + tl
                for oc in range(2):
                    ps = PS_A if (tl * 2 + oc) % 2 == 0 else PS_B
                    pst = ["PSA"] if (tl * 2 + oc) % 2 == 0 else ["PSB"]
                    for hc in range(16):
                        mm(ps[:, 0:512],
                           ATb[:, hc * 1024 + tl * 128: hc * 1024 + tl * 128 + 128],
                           AB(W2OF, 16384)[:, hc * 1024 + oc * 512: hc * 1024 + oc * 512 + 512],
                           start=(hc == 0), stop=(b2row is None and hc == 15),
                           r=attag(hc * 1024 + tl * 128, 128) + w2t, w=pst)
                    if b2row is not None:
                        mm(ps[:, 0:512], ONES1[:], b2row[0:1, oc * 512:(oc + 1) * 512],
                           start=False, stop=True, r=["ONES1", "B2ROW"], w=pst)
                    xr = xtg(tt * 1024 + oc * 512, tt * 1024 + oc * 512 + 512)
                    vop("dve", "tensor_tensor",
                        X[:, tt * 1024 + oc * 512: tt * 1024 + oc * 512 + 512],
                        X[:, tt * 1024 + oc * 512: tt * 1024 + oc * 512 + 512],
                        ps[:, 0:512], ALU.add, r=pst + xr, w=xr)

    def ttq(kc, tc, tt=None):
        if tc is not None:
            return TT[:, kc * 2048 + tc * 512: kc * 2048 + (tc + 1) * 512]
        return TT[:, kc * 2048 + tt * 128: kc * 2048 + tt * 128 + 128]

    def ttq_t(kc, tc, tt=None):
        if tc is not None:
            return ttg(kc, tc * 4, tc * 4 + 4)
        return ttg(kc, tt, tt + 1)

    def sublayer_attn_a():
        norm_to_TT("ln2", lambda tau: X[:, tau * 1024:(tau + 1) * 1024],
                   lambda tau: xtg(tau * 1024, tau * 1024 + 1024), NRMC_A, NRMY_A)
        for c in range(8):
            sset = c % 2
            qkv_pair(c, sset, dr["wq_a"], dr["wk_a"], dr["wv_a"], 8, 8,
                     ttq, ttq_t, ttq, ttq_t, qk_beta=True)
            for it in range(NT):
                for hh in range(2):
                    psel = it * 2 + hh

                    def evac(ps_att, rr, attw, sta, c=c, hh=hh, it=it):
                        vop("dve", "tensor_single_scalar", CMB[:, hh * 64:hh * 64 + 64],
                            ps_att, rr, ALU.mult, r=attw + sta, w=["CMB"])
                        if hh == 1:
                            gb = it % 2
                            trans(TRV[:, gb * 1024 + 512: gb * 1024 + 640], CMB[:, 0:128],
                                  r=["CMB", "IDN"], w=ptrtag(gb * 1024, gb * 1024 + 1024))
                            vop("dve", "tensor_copy",
                                ATb[:, c * 2048 + it * 128: c * 2048 + it * 128 + 128],
                                TRV[:, gb * 1024 + 512: gb * 1024 + 640],
                                r=ptrtag(gb * 1024, gb * 1024 + 1024),
                                w=attag(c * 2048 + it * 128, 128))
                    attention_step(QOF[sset] + (0 if hh == 0 else 2048),
                                   QOF[sset] + 4096 + (0 if hh == 0 else 2048),
                                   VOF[sset], hh, it, 4, None, (it == 0), psel, evac)
        wo_phase(dr["wo_a"], BOROW, True)
        build_bcast(0, SILOF)
        build_bcast(1, SILOF + 1024)
        norm_to_TT("lnaff", lambda tau: AB(HUOF, 16384)[:, tau * 1024:(tau + 1) * 1024],
                   lambda tau: atag(tau * 1024, 1024), NRMC_F, NRMY_F,
                   gbc=(AB(SILOF, 1024), AB(SILOF + 1024, 1024)),
                   gbc_tags=atag(SILOF, 2048))
        mlp_halves(dr["w1_a"], dr["w2_a"], True, True, B2ROW, None)

    def sublayer_ffn():
        build_bcast(2, SILOF)
        norm_to_TT("lnrms", lambda tau: X[:, tau * 1024:(tau + 1) * 1024],
                   lambda tau: xtg(tau * 1024, tau * 1024 + 1024), NRMC_A, NRMY_A,
                   gbc=(AB(SILOF, 1024),), gbc_tags=atag(SILOF, 1024))
        dma("gd", AB(WSOF[0], 4096).rearrange("p (lc k) -> p lc k", lc=4),
            dr["wdkv"].rearrange("(lc p) k -> p lc k", p=128), w=atag(WSOF[0], 4096))
        for lc in range(4):
            for tc in range(4):
                for kc in range(8):
                    mm(PS_SC[:, tc * 512:(tc + 1) * 512],
                       AB(WSOF[0], 4096)[:, lc * 1024 + kc * 128: lc * 1024 + kc * 128 + 128],
                       TT[:, kc * 2048 + tc * 512: kc * 2048 + (tc + 1) * 512],
                       start=(kc == 0), stop=(kc == 7),
                       r=atag(WSOF[0], 4096) + ttg(kc, tc * 4, tc * 4 + 4),
                       w=tgr("PSSC", tc * 512, tc * 512 + 512))
            vop("dve", "tensor_copy", AB(CTOF + lc * 2048, 2048), PS_SC[:],
                r=tgr("PSSC", 0, 2048), w=atag(CTOF + lc * 2048, 2048))

        def ctq(kc, tc, tt=None):
            if tc is not None:
                return AB(CTOF + kc * 2048, 2048)[:, tc * 512:(tc + 1) * 512]
            return AB(CTOF + kc * 2048, 2048)[:, tt * 128:tt * 128 + 128]

        def ctq_t(kc, tc, tt=None):
            return atag(CTOF + kc * 2048, 2048)

        for c in range(8):
            sset = c % 2
            qkv_pair(c, sset, dr["wq_f"], dr["wuk"], dr["wuv"], 8, 4,
                     ttq, ttq_t, ctq, ctq_t, qscale=0.125)
            for it in range(NT):
                for hh in range(2):
                    psel = it * 2 + hh

                    def evac(ps_att, rr, attw, sta, c=c, hh=hh, it=it):
                        vop("dve", "tensor_single_scalar", CMB[:, hh * 64:hh * 64 + 64],
                            ps_att, rr, ALU.mult, r=attw + sta, w=["CMB"])
                        if hh == 1:
                            gb = it % 2
                            trans(TRV[:, gb * 1024 + 512: gb * 1024 + 640], CMB[:, 0:128],
                                  r=["CMB", "IDN"], w=ptrtag(gb * 1024, gb * 1024 + 1024))
                            vop("dve", "tensor_copy",
                                ATb[:, c * 2048 + it * 128: c * 2048 + it * 128 + 128],
                                TRV[:, gb * 1024 + 512: gb * 1024 + 640],
                                r=ptrtag(gb * 1024, gb * 1024 + 1024),
                                w=attag(c * 2048 + it * 128, 128))
                    attention_step(QOF[sset] + (0 if hh == 0 else 2048),
                                   QOF[sset] + 4096 + (0 if hh == 0 else 2048),
                                   VOF[sset], hh, it, it // 4 + 1, it % 4, False, psel, evac)
        wo_phase(dr["wo_f"], None, True)
        build_bcast(3, SILOF)
        norm_to_TT("rmsg", lambda tau: AB(HUOF, 16384)[:, tau * 1024:(tau + 1) * 1024],
                   lambda tau: atag(tau * 1024, 1024), NRMC_F, NRMY_F,
                   gbc=(AB(SILOF, 1024),), gbc_tags=atag(SILOF, 1024))
        for tt in range(NT):
            for kc in range(8):
                mm(PS_A[:, 0:8], TT[:, kc * 2048 + tt * 128: kc * 2048 + tt * 128 + 128],
                   GWS[:, kc * 8:(kc + 1) * 8], start=(kc == 0), stop=(kc == 7),
                   r=ttg(kc, tt, tt + 1) + ["GWS"], w=["PSA"])
            stg = ["STg"]
            lg = ST[:, 36:44]
            m1 = ST[:, 44:45]
            z8 = ST[:, 45:53]
            i1 = ST[:, 53:61]
            v2 = ST[:, 61:62]
            rD = ST[:, 62:63]
            ez = ST[:, 36:44]
            vop("dve", "tensor_copy", lg, PS_A[:, 0:8], r=["PSA"], w=stg)
            vop("dve", "reduce_max", m1, lg, AX.X, r=stg, w=stg)
            vop("dve", "tensor_single_scalar", z8, lg, m1, ALU.subtract, r=stg, w=stg)
            vop("dve", "tensor_single_scalar", i1, z8, -1e-20, ALU.is_ge, r=stg, w=stg)
            vop("dve", "scalar_tensor_tensor", lg, i1, -1e9, z8, ALU.mult, ALU.add,
                r=stg, w=stg)
            vop("dve", "reduce_max", v2, lg, AX.X, r=stg, w=stg)
            vop("dve", "tensor_single_scalar", lg, lg, v2, ALU.subtract, r=stg, w=stg)
            vop("dve", "tensor_single_scalar", lg, lg, -1e-20, ALU.is_ge, r=stg, w=stg)
            vop("dve", "tensor_tensor", i1, i1, lg, ALU.add, r=stg, w=stg)
            ev2 = ST[:, 63:64]
            act(ez, z8, AF.Exp, r=stg, w=stg)
            act(ev2, v2, AF.Exp, r=stg, w=stg)
            vop("dve", "tensor_single_scalar", ev2, ev2, 1.0, ALU.add, r=stg, w=stg)
            vop("dve", "reciprocal", rD, ev2, r=stg, w=stg)
            vop("dve", "tensor_tensor", ez, ez, i1, ALU.mult, r=stg, w=stg)
            vop("dve", "tensor_single_scalar", CMB[:, 0:8], ez, rD, ALU.mult, r=stg,
                w=["CMB"])
            gb = tt % 2
            trans(TRV[0:8, gb * 1024: gb * 1024 + 128], CMB[:, 0:8], r=["CMB", "IDN"],
                  w=ptrtag(gb * 1024, gb * 1024 + 1024))
            vop("dve", "tensor_copy", CT8[0:8, tt * 128:(tt + 1) * 128],
                TRV[0:8, gb * 1024: gb * 1024 + 128],
                r=ptrtag(gb * 1024, gb * 1024 + 1024), w=["CT8"])
        for e in range(NEXP):
            dma("sd", STGE[0:1, :], CT8[e:e + 1, :], r=["CT8"], w=["STGE"])
            mlp_halves(dr["w1_m"][e * 2048:(e + 1) * 2048, :],
                       dr["w2_m"][e * 2048:(e + 1) * 2048, :],
                       False, False, None, STGE)

    def sublayer_attn_b(qkv_only=False, npairs=8):
        norm_to_TT("ln", lambda tau: X[:, tau * 1024:(tau + 1) * 1024],
                   lambda tau: xtg(tau * 1024, tau * 1024 + 1024), NRMC_A, NRMY_A)
        for c in range(npairs):
            sset = c % 2
            qkv_pair(c, sset, dr["wq_b"], dr["wk_b"], dr["wv_b"], 8, 8,
                     ttq, ttq_t, ttq, ttq_t, qscale=0.125)
            if qkv_only:
                continue
            for hh in range(2):
                qof = QOF[sset] + (0 if hh == 0 else 2048)
                kof = QOF[sset] + 4096 + (0 if hh == 0 else 2048)
                vof = VOF[sset]
                qtags = atag(qof, 2048)
                ktags = atag(kof, 2048)
                vtags = atag(vof, 2048)
                h = 2 * c + hh
                for w0 in range(0, NT, 2):
                    for wi in range(2):
                        w = w0 + wi
                        jlo = max(0, (w - 1) * 128)
                        jhi = min(S, (w + 2) * 128)
                        nj = jhi - jlo
                        pof = wi * 1024
                        rows = slice(0, 64) if wi == 0 else slice(64, 128)
                        mm(PS_SC[:, pof:pof + nj],
                           AB(qof, 2048)[rows, w * 128:(w + 1) * 128],
                           AB(kof, 2048)[rows, jlo:jhi], True, True,
                           tp=(0, 0) if wi == 0 else (64, 0),
                           r=qtags + ktags, w=tgr("PSSC", pof, pof + nj))
                    for wi in range(2):
                        w = w0 + wi
                        jlo = max(0, (w - 1) * 128)
                        jhi = min(S, (w + 2) * 128)
                        nj = jhi - jlo
                        njt = nj // 128
                        pof = wi * 1024
                        scw = tgr("PSSC", pof, pof + nj)
                        sta = ["STa%d" % wi]
                        nm = ST[:, 10 + wi:11 + wi]
                        ssum = ST[:, 20 + wi:21 + wi]
                        rr = ST[:, 22 + wi:23 + wi]
                        vop("dve", "reduce_max", nm,
                            PS_SC[:, pof:pof + nj], AX.X, None, None, True, r=scw, w=sta)
                        psof = PSOF[wi]
                        act(AB(psof, 512)[:, 0:nj], PS_SC[:, pof:pof + nj], AF.Exp,
                            bias=nm, accum=ssum, r=scw + sta, w=atag(psof, 512) + sta)
                        vop("dve", "reciprocal", rr, ssum, r=sta, w=sta)
                        for q in range(njt):
                            trans(TRV[:, wi * 1024 + q * 128: wi * 1024 + (q + 1) * 128],
                                  AB(psof, 512)[:, q * 128:(q + 1) * 128],
                                  r=atag(psof, 512) + ["IDN"],
                                  w=ptrtag(wi * 1024, wi * 1024 + 1024))
                        ptof = PTOF[wi]
                        vop("dve", "tensor_copy",
                            AB(ptof, 512)[:, 0:njt * 128],
                            TRV[:, wi * 1024:wi * 1024 + njt * 128],
                            r=ptrtag(wi * 1024, wi * 1024 + 1024), w=atag(ptof, 512))
                        ps_att = (PS_A if wi == 0 else PS_B)[:, 0:64]
                        attw = ["PSA" if wi == 0 else "PSB"]
                        for q in range(njt):
                            jt = jlo // 128 + q
                            mm(ps_att, AB(ptof, 512)[:, q * 128:(q + 1) * 128],
                               AB(vof, 2048)[:, jt * 128 + hh * 64: jt * 128 + hh * 64 + 64],
                               start=(q == 0), stop=(q == njt - 1),
                               r=atag(ptof, 512) + vtags, w=attw)
                        xr = xtg(w * 1024 + h * 64, w * 1024 + h * 64 + 64)
                        vop("dve", "scalar_tensor_tensor",
                            X[:, w * 1024 + h * 64: w * 1024 + h * 64 + 64],
                            ps_att, rr, X[:, w * 1024 + h * 64: w * 1024 + h * 64 + 64],
                            ALU.mult, ALU.add, r=attw + sta + xr, w=xr)

    if "n1" in PARTS:
        norm_to_TT("ln2", lambda tau: X[:, tau * 1024:(tau + 1) * 1024],
                   lambda tau: xtg(tau * 1024, tau * 1024 + 1024), NRMC_A, NRMY_A)
    if "nl" in PARTS:
        norm_to_TT("ln", lambda tau: X[:, tau * 1024:(tau + 1) * 1024],
                   lambda tau: xtg(tau * 1024, tau * 1024 + 1024), NRMC_A, NRMY_A)
    if any(isinstance(q, str) and q.startswith("nt") for q in PARTS):
        stage = [q for q in PARTS if isinstance(q, str)][0]
        tau = 0
        rot = 0
        Cof = NRMC_A
        yof = NRMY_A
        YH = AB(yof, 1024)
        ytags = atag(yof, 1024)
        C, ctags, rs, t = ln_tile(X[:, 0:1024], xtg(0, 1024), Cof, rot, 1e-5, "STn")
        if stage >= "ntB":
            vop("dve", "tensor_single_scalar", YH, C, rs, ALU.mult, r=ctags + t, w=ytags)
        if stage >= "ntC":
            for c in range(NC):
                g = c % 8
                trw = TRV[:, g * 128:(g + 1) * 128]
                trans(trw, YH[:, c * 128:(c + 1) * 128], r=ytags + ["IDN"],
                      w=ptrtag(g * 128, g * 128 + 128))
                if stage >= "ntD":
                    vop("dve", "tensor_copy",
                        TT[:, c * 2048 + tau * 128: c * 2048 + tau * 128 + 128], trw,
                        r=ptrtag(g * 128, g * 128 + 128), w=["TT%d_%d" % (c, tau)])
    if 1 in PARTS:
        sublayer_attn_a()
    if 2 in PARTS:
        sublayer_ffn()
    if 3 in PARTS:
        sublayer_attn_b()
    if "3q" in PARTS:
        sublayer_attn_b(qkv_only=True, npairs=2)
    if "3a" in PARTS:
        sublayer_attn_b(npairs=1)
    if "3b" in PARTS:
        sublayer_attn_b(npairs=2)
    if "3c" in PARTS:
        sublayer_attn_b(npairs=6)
    if 4 in PARTS:
        sublayer_ffn()

    dma("gd", out, X[:], r=xtg(0, 16384), w=["OUT"])

    final = {s: p.cnt[s] for s in STREAMS}

    def replay(section, eng):
        for st, waits, fn in p.items[section]:
            for s2, q in waits:
                eng.wait_ge(sems[s2], q * INC[s2])
            fn(eng).then_inc(sems[st], INC[st])

    with es:
        with nc.Block() as block:
            @block.sync
            def _(eng):
                replay("sync", eng)
                for s in STREAMS:
                    if final[s]:
                        eng.wait_ge(sems[s], final[s] * INC[s])

            @block.gpsimd
            def _(eng):
                replay("gpsimd", eng)

            @block.vector
            def _(eng):
                replay("vector", eng)

            @block.scalar
            def _(eng):
                replay("scalar", eng)

            @block.tensor
            def _(eng):
                replay("tensor", eng)

    return nc


_CACHE = {}


def kernel(**inputs):
    d = _repack(inputs)
    key = "prog" + str(PARTS)
    if key not in _CACHE:
        _CACHE[key] = build_program()
    nc = _CACHE[key]
    shared = {k: v for k, v in d.items() if k != "x"}
    in_maps = [dict(shared, x=d["x"][b]) for b in range(8)]
    res = run_bass_kernel_spmd(nc, in_maps, core_ids=list(range(8)))
    outs = []
    for b in range(8):
        o = np.asarray(res.results[b]["out"]).reshape(128, NT, D).transpose(1, 0, 2)
        outs.append(o.reshape(S, D))
    return np.stack(outs).astype(np.float32)


# version 03
# speedup vs baseline: 2.0618x; 2.0618x over previous
import sys
import numpy as np
import ml_dtypes

try:
    import concourse.bass as bass
except Exception:
    sys.path.insert(0, "/opt/trn_rl_repo")
    import concourse.bass as bass
import concourse.mybir as mybir
from concourse.bass_utils import run_bass_kernel_spmd

F32 = mybir.dt.float32
BF16 = mybir.dt.bfloat16
AF = mybir.ActivationFunctionType
ALU = mybir.AluOpType
AX = mybir.AxisListType
BFd = ml_dtypes.bfloat16

D = 1024
S = 2048
NT = 16
NC = 8
NEXP = 8

PARTS = (1, 2, 3, 4)

STREAMS = ("pe", "dve", "act", "pool", "sd", "gd")
SECTION = {"pe": "tensor", "dve": "vector", "act": "scalar", "pool": "gpsimd",
           "sd": "sync", "gd": "gpsimd"}
INC = {"pe": 1, "dve": 1, "act": 1, "pool": 1, "sd": 16, "gd": 16}
SAME_ENG_SAFE = 14


class Prog:
    def __init__(self):
        self.items = {"tensor": [], "vector": [], "scalar": [], "gpsimd": [], "sync": []}
        self.cnt = {s: 0 for s in STREAMS}
        self.lastw = {}
        self.lastr = {}
        self.sat = {s: {} for s in STREAMS}

    def emit(self, st, fn, r=(), w=()):
        deps = {}
        for t in r:
            lw = self.lastw.get(t)
            if lw is not None and deps.get(lw[0], -1) < lw[1]:
                deps[lw[0]] = lw[1]
        for t in w:
            lw = self.lastw.get(t)
            if lw is not None and deps.get(lw[0], -1) < lw[1]:
                deps[lw[0]] = lw[1]
            for s2, q in self.lastr.get(t, {}).items():
                if deps.get(s2, -1) < q:
                    deps[s2] = q
        self.cnt[st] += 1
        seq = self.cnt[st]
        waits = []
        if st in ("sd", "gd") and seq > 1:
            if self.sat[st].get(st, 0) < seq - 1:
                self.sat[st][st] = seq - 1
                waits.append((st, seq - 1))
        for s2, q in deps.items():
            if s2 == st:
                if st in ("dve", "act", "pool") and (seq - 1) - q < SAME_ENG_SAFE:
                    if self.sat[st].get(st, 0) < q:
                        self.sat[st][st] = q
                        waits.append((st, q))
                continue
            if self.sat[st].get(s2, 0) >= q:
                continue
            self.sat[st][s2] = q
            waits.append((s2, q))
        self.items[SECTION[st]].append((st, waits, fn))
        for t in w:
            self.lastw[t] = (st, seq)
            self.lastr[t] = {}
        for t in r:
            d = self.lastr.setdefault(t, {})
            if d.get(st, 0) < seq:
                d[st] = seq
        return seq


def tgr(name, lo, hi, blk=512):
    return [name + str(i) for i in range(lo // blk, (hi + blk - 1) // blk)]


def _repack(inputs):
    f = np.ascontiguousarray
    d = {}
    x = np.asarray(inputs["L0_attn_a_x"], np.float32)
    d["x"] = [f(x[b].reshape(NT, 128, D).transpose(1, 0, 2).reshape(128, NT * D))
              for b in range(8)]

    def colpack(Wm, kb=128):
        K, M = Wm.shape
        nkc = K // kb
        npair = M // 128
        return f(Wm.reshape(nkc, kb, npair, 128).transpose(2, 1, 0, 3)
                 .reshape(npair * kb, nkc * 128))

    d["wq_a"] = colpack(np.asarray(inputs["L0_attn_a_Wq"], np.float32))
    d["wk_a"] = colpack(np.asarray(inputs["L0_attn_a_Wk"], np.float32))
    d["wv_a"] = colpack(np.asarray(inputs["L0_attn_a_Wv"], np.float32))
    d["wo_a"] = f(np.asarray(inputs["L0_attn_a_Wo"], np.float32))
    W1 = np.asarray(inputs["L0_attn_a_W1"], np.float32)
    d["w1_a"] = f(W1.reshape(8, 128, 16, 128).transpose(2, 1, 0, 3).reshape(2048, 1024))
    d["w2_a"] = f(np.asarray(inputs["L0_attn_a_W2"], np.float32))

    d["wq_f"] = colpack(np.asarray(inputs["L0_ffn_Wq"], np.float32))
    Wdkv = np.asarray(inputs["L0_ffn_Wdkv"], np.float32)
    d["wdkv"] = f(Wdkv.reshape(8, 128, 4, 128).transpose(2, 1, 0, 3).reshape(512, 1024))
    Wuk = np.asarray(inputs["L0_ffn_Wuk"], np.float32)
    d["wuk"] = f(Wuk.reshape(4, 128, 8, 128).transpose(2, 1, 0, 3).reshape(1024, 512))
    Wuv = np.asarray(inputs["L0_ffn_Wuv"], np.float32)
    d["wuv"] = f(Wuv.reshape(4, 128, 8, 128).transpose(2, 1, 0, 3).reshape(1024, 512))
    d["wo_f"] = f(np.asarray(inputs["L0_ffn_Wo"], np.float32))

    wqkv = np.asarray(inputs["L0_attn_b_w_qkv"], np.float32)
    d["wq_b"] = colpack(wqkv[:, 0:1024])
    d["wk_b"] = colpack(wqkv[:, 1024:2048])
    d["wv_b"] = colpack(wqkv[:, 2048:3072])

    w1m = np.asarray(inputs["L0_ffn_w1"], np.float32)
    d["w1_m"] = f(w1m.reshape(NEXP, 8, 128, 16, 128).transpose(0, 3, 2, 1, 4)
                  .reshape(NEXP * 2048, 1024))
    d["w2_m"] = f(np.asarray(inputs["L0_ffn_w2"], np.float32).reshape(NEXP * 2048, D))

    d["gws"] = f(np.asarray(inputs["L0_ffn_gate_w"], np.float32)
                 .reshape(8, 128, 8).transpose(1, 0, 2).reshape(128, 64))

    def pcol(v):
        return np.asarray(v, np.float32).reshape(8, 128).T

    sm = np.zeros((128, 96), np.float32)
    sm[:, 0:8] = pcol(inputs["L0_attn_a_lnq_s"]); sm[:, 8:16] = pcol(inputs["L0_attn_a_lnq_b"])
    sm[:, 16:24] = pcol(inputs["L0_attn_a_lnk_s"]); sm[:, 24:32] = pcol(inputs["L0_attn_a_lnk_b"])
    sm[:, 32:40] = pcol(inputs["L0_attn_a_lnv_s"]); sm[:, 40:48] = pcol(inputs["L0_attn_a_lnv_b"])
    sm[:, 48:56] = pcol(inputs["L0_attn_a_bq"]); sm[:, 56:64] = pcol(inputs["L0_attn_a_bk"])
    sm[:, 64:72] = pcol(inputs["L0_attn_a_bv"])
    sm[:, 72:88] = np.asarray(inputs["L0_attn_a_b1"], np.float32).reshape(16, 128).T
    d["smalls"] = f(sm)

    rows = np.zeros((8, 1024), np.float32)
    rows[0] = np.asarray(inputs["L0_attn_a_lnm_s"], np.float32)
    rows[1] = np.asarray(inputs["L0_attn_a_lnm_b"], np.float32)
    rows[2] = np.asarray(inputs["L0_ffn_g1"], np.float32)
    rows[3] = np.asarray(inputs["L0_ffn_g2"], np.float32)
    rows[4] = np.asarray(inputs["L0_attn_a_bo"], np.float32)
    rows[5] = np.asarray(inputs["L0_attn_a_b2"], np.float32)
    d["rows"] = f(rows)

    d["idn"] = np.eye(128, dtype=np.float32).astype(BFd)
    mk = np.zeros((128, 4, 512), np.float32)
    ii = np.arange(128)[:, None]
    jj = np.arange(512)[None, :]
    for rr in range(4):
        mk[:, rr, :] = np.where(jj <= rr * 128 + ii, 0.0, -30000.0)
    d["maskc"] = f(mk.reshape(128, 2048)).astype(BFd)
    return d


def build_program():
    nc = bass.Bass("TRN2", target_bir_lowering=False)
    dr = {}
    dr["x"] = nc.dram_tensor("x", [128, NT * D], F32, kind="ExternalInput").ap()
    for n, shape in [("wq_a", [1024, 1024]), ("wk_a", [1024, 1024]), ("wv_a", [1024, 1024]),
                     ("wo_a", [1024, 1024]), ("w1_a", [2048, 1024]), ("w2_a", [2048, 1024]),
                     ("wq_f", [1024, 1024]), ("wdkv", [512, 1024]), ("wuk", [1024, 512]),
                     ("wuv", [1024, 512]), ("wo_f", [1024, 1024]),
                     ("wq_b", [1024, 1024]), ("wk_b", [1024, 1024]), ("wv_b", [1024, 1024]),
                     ("w1_m", [NEXP * 2048, 1024]), ("w2_m", [NEXP * 2048, 1024]),
                     ("gws", [128, 64]), ("smalls", [128, 96]), ("rows", [8, 1024])]:
        dr[n] = nc.dram_tensor(n, shape, F32, kind="ExternalInput").ap()
    dr["idn"] = nc.dram_tensor("idn", [128, 128], BF16, kind="ExternalInput").ap()
    dr["maskc"] = nc.dram_tensor("maskc", [128, 2048], BF16, kind="ExternalInput").ap()
    out = nc.dram_tensor("out", [128, NT * D], F32, kind="ExternalOutput").ap()

    from contextlib import ExitStack
    es = ExitStack()
    X = es.enter_context(nc.sbuf_tensor("X", [128, 16384], BF16))
    TT = es.enter_context(nc.sbuf_tensor("TT", [128, 16384], BF16))
    ATb = es.enter_context(nc.sbuf_tensor("ATb", [128, 16384], BF16))
    AR = es.enter_context(nc.sbuf_tensor("AR", [128, 40960], BF16))
    IDN = es.enter_context(nc.sbuf_tensor("IDN", [128, 128], BF16))
    MKC = es.enter_context(nc.sbuf_tensor("MKC", [128, 2048], BF16))
    SM = es.enter_context(nc.sbuf_tensor("SM", [128, 96], F32))
    SMB = es.enter_context(nc.sbuf_tensor("SMB", [128, 96], BF16))
    GWS = es.enter_context(nc.sbuf_tensor("GWS", [128, 64], BF16))
    ST = es.enter_context(nc.sbuf_tensor("ST", [128, 64], F32))
    STA = es.enter_context(nc.sbuf_tensor("STA", [128, 40], F32))
    ONES1 = es.enter_context(nc.sbuf_tensor("ONES1", [1, 128], BF16))
    EPS = es.enter_context(nc.sbuf_tensor("EPS", [128, 2], F32))
    BOROW = es.enter_context(nc.sbuf_tensor("BOROW", [1, 1024], BF16))
    B2ROW = es.enter_context(nc.sbuf_tensor("B2ROW", [1, 1024], BF16))
    RSTG = es.enter_context(nc.sbuf_tensor("RSTG", [1, 1024], BF16))
    STGE = es.enter_context(nc.sbuf_tensor("STGE", [1, 2048], BF16))
    BVT = es.enter_context(nc.sbuf_tensor("BVT", [1, 128], BF16))
    BVC = es.enter_context(nc.sbuf_tensor("BVC", [128, 1], BF16))
    CT8 = es.enter_context(nc.sbuf_tensor("CT8", [8, 2048], BF16))
    CMB = es.enter_context(nc.sbuf_tensor("CMB", [128, 128], BF16))
    PS_SC = es.enter_context(nc.psum_tensor("PS_SC", [128, 2048], F32))
    PS_TR = es.enter_context(nc.psum_tensor("PS_TR", [128, 1024], F32))
    PS_A = es.enter_context(nc.psum_tensor("PS_A", [128, 512], F32))
    PS_B = es.enter_context(nc.psum_tensor("PS_B", [128, 512], F32))
    sems = {s: es.enter_context(nc.semaphore("sem_" + s)) for s in STREAMS}

    p = Prog()

    def AB(off, n):
        return AR[:, off:off + n]

    def AF32(offb, nf):
        return AR[:].bitcast(F32)[:, offb // 2: offb // 2 + nf]

    def atag(off_bf, n_bf):
        return tgr("AR", off_bf, off_bf + n_bf)

    def attag(off, n):
        return tgr("AT", off, off + n, 1024)

    def xtg(off, n):
        return tgr("X", off, off + n)

    def ttg(kc, tlo, thi):
        return ["TT%d_%d" % (kc, t) for t in range(tlo, thi)]

    def ptrtag(bflo, bfhi):
        return tgr("PTRB", bflo, bfhi, 1024)

    QOF = [0, 8192]
    VOF = [16384, 18432]
    PSOF = [20480, 22528]
    PTOF = [24576, 25088]
    WSOF = [25728, 28800]
    CTOF = 32768
    WOOF = 32768
    W2OF = 0
    W1OF = 16384
    SILOF = 32768
    HUOF = 0
    NRMC_A = 0
    NRMY_A = 8192
    NRMC_F = 24576
    NRMY_F = 20480

    def dma(st, o, i, r=(), w=()):
        p.emit(st, (lambda o=o, i=i: (lambda e: e.dma_start(out=o, in_=i)))(), r=r, w=w)

    def mm(psum, lhsT, rhs, start, stop, tp=None, r=(), w=()):
        p.emit("pe", (lambda psum=psum, lhsT=lhsT, rhs=rhs, start=start, stop=stop, tp=tp:
                      (lambda e: e.matmul(psum, lhsT, rhs, start=start, stop=stop,
                                          tile_position=tp)))(), r=r, w=w)

    def trans(psum, inp, r=(), w=()):
        p.emit("pe", (lambda psum=psum, inp=inp:
                      (lambda e: e.transpose(psum, inp, IDN[:])))(), r=r, w=w)

    def act(o, i, func, bias=None, scale=1.0, accum=None, r=(), w=()):
        kw = {}
        if bias is not None:
            kw["bias"] = bias
        if accum is not None:
            kw["accum_out"] = accum
        p.emit("act", (lambda o=o, i=i, func=func, scale=scale, kw=kw:
                       (lambda e: e.activation(o, i, func, scale=scale, **kw)))(), r=r, w=w)

    def vop(st, meth, *args, r=(), w=()):
        p.emit(st, (lambda meth=meth, args=args:
                    (lambda e: getattr(e, meth)(*args)))(), r=r, w=w)

    TRV = PS_TR[:].bitcast(BF16)

    dma("gd", X[:], dr["x"], w=xtg(0, 16384))
    dma("sd", IDN[:], dr["idn"], w=["IDN"])
    dma("sd", MKC[:], dr["maskc"], w=["MKC"])
    dma("sd", SM[:], dr["smalls"], w=["SM"])
    dma("gd", GWS[:], dr["gws"], w=["GWS"])
    dma("gd", BOROW[:], dr["rows"][4:5, :], w=["BOROW"])
    dma("gd", B2ROW[:], dr["rows"][5:6, :], w=["B2ROW"])
    vop("pool", "memset", ONES1[:], 1.0, w=["ONES1"])
    vop("pool", "tensor_copy", SMB[:], SM[:], r=["SM"], w=["SMB"])
    vop("pool", "memset", EPS[:, 0:1], 1e-5, w=["EPS"])
    vop("pool", "memset", EPS[:, 1:2], 1e-6, w=["EPS"])

    def ln_tile(src, stags, Cof, rot, eps, stn):
        C = AF32(Cof, 1024)
        ctags = atag(Cof, 2048)
        C2of = Cof + 4096
        C2 = AF32(C2of, 1024)
        c2tags = atag(C2of, 2048)
        mu = ST[:, 0 + rot:1 + rot] if stn == "STn" else ST[:, 30 + rot:31 + rot]
        vv = ST[:, 2 + rot:3 + rot] if stn == "STn" else ST[:, 32 + rot:33 + rot]
        rs = ST[:, 4 + rot:5 + rot] if stn == "STn" else ST[:, 34 + rot:35 + rot]
        t = [stn + str(rot)]
        vop("dve", "reduce_sum", mu, src, AX.X, r=stags, w=t)
        vop("dve", "tensor_single_scalar", mu, mu, 1.0 / D, ALU.mult, r=t, w=t)
        vop("dve", "scalar_tensor_tensor", C, src, mu, src, ALU.subtract, ALU.bypass,
            r=stags + t, w=ctags)
        act(C2, C, AF.Square, accum=vv, r=ctags + t, w=c2tags + t)
        vop("dve", "tensor_single_scalar", vv, vv, 1.0 / D, ALU.mult, r=t, w=t)
        act(rs, vv, AF.Sqrt, bias=EPS[:, 0:1] if eps == 1e-5 else EPS[:, 1:2],
            r=t + ["EPS"], w=t)
        vop("dve", "reciprocal", rs, rs, r=t, w=t)
        return C, ctags, rs, t

    def rms_tile(src, stags, Cof, rot, eps, stn="STn"):
        C2of = Cof + 4096
        C2 = AF32(C2of, 1024)
        vv = ST[:, 2 + rot:3 + rot]
        rs = ST[:, 4 + rot:5 + rot]
        t = [stn + str(rot)]
        act(C2, src, AF.Square, accum=vv, r=stags + t, w=atag(C2of, 2048) + t)
        vop("dve", "tensor_single_scalar", vv, vv, 1.0 / D, ALU.mult, r=t, w=t)
        act(rs, vv, AF.Sqrt, bias=EPS[:, 0:1] if eps == 1e-5 else EPS[:, 1:2],
            r=t + ["EPS"], w=t)
        vop("dve", "reciprocal", rs, rs, r=t, w=t)
        return rs, t

    def transpose_to_TT(YH, ytags, tau, dst=None):
        dstt = TT if dst is None else dst
        t3 = dstt[:].rearrange("p (c m) -> p c m", c=8)
        for grp in range(2):
            bank = (tau * 2 + grp) % 2
            for q in range(4):
                c = grp * 4 + q
                trw = TRV[:, bank * 1024 + q * 128: bank * 1024 + (q + 1) * 128]
                trans(trw, YH[:, c * 128:(c + 1) * 128], r=ytags + ["IDN"],
                      w=ptrtag(bank * 1024, bank * 1024 + 1024))
            vop("dve", "tensor_copy",
                t3[:, grp * 4:grp * 4 + 4, tau * 128: tau * 128 + 128],
                TRV[:, bank * 1024: bank * 1024 + 512].rearrange("p (c m) -> p c m", c=4),
                r=ptrtag(bank * 1024, bank * 1024 + 1024),
                w=["TT%d_%d" % (grp * 4 + q, tau) for q in range(4)])

    def norm_to_TT(kind, src_fn, stags_fn, cbase, ybase, gbc=None, gbc_tags=(), eps=1e-5):
        for tau in range(NT):
            rot = tau % 2
            Cof = cbase + rot * 2048
            yof = ybase + rot * 2048
            YH = AB(yof, 1024)
            ytags = atag(yof, 1024)
            src = src_fn(tau)
            stags = stags_fn(tau)
            if kind == "ln":
                C, ctags, rs, t = ln_tile(src, stags, Cof, rot, eps, "STn")
                vop("dve", "tensor_single_scalar", YH, C, rs, ALU.mult, r=ctags + t, w=ytags)
            elif kind == "ln2":
                C, ctags, rs, t = ln_tile(src, stags, Cof, rot, eps, "STn")
                vop("dve", "tensor_single_scalar", C, C, rs, ALU.mult, r=ctags + t, w=ctags)
                C, ctags, rs, t = ln_tile(C, ctags, Cof, rot, eps, "STn")
                vop("dve", "tensor_single_scalar", YH, C, rs, ALU.mult, r=ctags + t, w=ytags)
            elif kind == "lnaff":
                C, ctags, rs, t = ln_tile(src, stags, Cof, rot, eps, "STn")
                vop("dve", "tensor_single_scalar", C, C, rs, ALU.mult, r=ctags + t, w=ctags)
                vop("pool", "tensor_tensor", C, C, gbc[0], ALU.mult,
                    r=ctags + list(gbc_tags), w=ctags)
                vop("pool", "tensor_tensor", YH, C, gbc[1], ALU.add,
                    r=ctags + list(gbc_tags), w=ytags)
            elif kind == "lnrms":
                C, ctags, rs, t = ln_tile(src, stags, Cof, rot, 1e-5, "STn")
                vop("dve", "tensor_single_scalar", C, C, rs, ALU.mult, r=ctags + t, w=ctags)
                rs2, t2 = rms_tile(C, ctags, Cof, rot, 1e-6)
                vop("dve", "tensor_single_scalar", C, C, rs2, ALU.mult, r=ctags + t2, w=ctags)
                vop("pool", "tensor_tensor", YH, C, gbc[0], ALU.mult,
                    r=ctags + list(gbc_tags), w=ytags)
            elif kind == "rmsg":
                rs2, t2 = rms_tile(src, stags, Cof, rot, 1e-6)
                vop("dve", "tensor_single_scalar", AF32(Cof, 1024), src, rs2, ALU.mult,
                    r=stags + t2, w=atag(Cof, 2048))
                vop("pool", "tensor_tensor", YH, AF32(Cof, 1024), gbc[0], ALU.mult,
                    r=atag(Cof, 2048) + list(gbc_tags), w=ytags)
            transpose_to_TT(YH, ytags, tau)

    def build_bcast(row_idx, dstof):
        dma("gd", RSTG[0:1, :], dr["rows"][row_idx:row_idx + 1, :], r=["RSTG"], w=["RSTG"])
        for q in range(2):
            mm(PS_TR[:, q * 512:(q + 1) * 512], ONES1[:],
               RSTG[0:1, q * 512:(q + 1) * 512], True, True,
               r=["RSTG", "ONES1"], w=ptrtag(q * 1024, q * 1024 + 1024))
        vop("dve", "tensor_copy", AB(dstof, 1024), PS_TR[:, 0:1024],
            r=ptrtag(0, 2048), w=atag(dstof, 1024))

    def qkv_pair(c, sset, wq_dr, wk_dr, wv_dr, nkc_q, nkc_kv, srcq, srcq_t, srckv, srckv_t,
                 qk_beta=False, qscale=None):
        wof = WSOF[sset]
        qslot, kslot, vslot = wof, wof + 1024, wof + 2048
        tq = atag(qslot, 1024)
        tk = atag(kslot, 1024)
        tv = atag(vslot, 1024)
        dma("gd", AB(qslot, nkc_q * 128), wq_dr[c * 128:(c + 1) * 128, :], w=tq)
        dma("gd", AB(kslot, nkc_kv * 128), wk_dr[c * 128:(c + 1) * 128, :], w=tk)
        dma("gd", AB(vslot, nkc_kv * 128), wv_dr[c * 128:(c + 1) * 128, :], w=tv)
        rot = c % 2
        bq = ST[:, 24 + rot:25 + rot]
        bk = ST[:, 26 + rot:27 + rot]
        bv = ST[:, 28 + rot:29 + rot]
        stb = ["STb%d" % rot]
        if qk_beta:
            for (slot, stags, lns, lnb, bcol, beta, extra) in (
                    (qslot, tq, 0, 8, 48, bq, 1.0 / 32.0),
                    (kslot, tk, 16, 24, 56, bk, None),
                    (vslot, tv, 32, 40, 64, bv, None)):
                for kc in range(8):
                    mm(PS_A[:, 64:65], AB(slot, 1024)[:, kc * 128:(kc + 1) * 128],
                       SMB[:, lnb + kc:lnb + kc + 1], start=(kc == 0), stop=(kc == 7),
                       r=stags + ["SMB"], w=["PSA"])
                vop("dve", "tensor_tensor", beta, PS_A[:, 64:65],
                    SM[:, bcol + c:bcol + c + 1], ALU.add, r=["PSA", "SM"], w=stb)
                if extra is not None:
                    vop("dve", "tensor_single_scalar", beta, beta, extra, ALU.mult,
                        r=stb, w=stb)
                for kc in range(8):
                    sc = SM[:, lns + kc:lns + kc + 1]
                    vop("dve", "tensor_single_scalar",
                        AB(slot, 1024)[:, kc * 128:(kc + 1) * 128],
                        AB(slot, 1024)[:, kc * 128:(kc + 1) * 128], sc, ALU.mult,
                        r=stags + ["SM"], w=stags)
                    if extra is not None:
                        vop("dve", "tensor_single_scalar",
                            AB(slot, 1024)[:, kc * 128:(kc + 1) * 128],
                            AB(slot, 1024)[:, kc * 128:(kc + 1) * 128], extra, ALU.mult,
                            r=stags, w=stags)
        elif qscale is not None:
            vop("dve", "tensor_single_scalar", AB(qslot, nkc_q * 128),
                AB(qslot, nkc_q * 128), qscale, ALU.mult, r=tq, w=tq)
        qof = QOF[sset]
        for (slot, stags, dof, nkc, srcf, srcft, beta) in (
                (qslot, tq, qof, nkc_q, srcq, srcq_t, bq if qk_beta else None),
                (kslot, tk, qof + 4096, nkc_kv, srckv, srckv_t, bk if qk_beta else None)):
            for tc in range(4):
                for kc in range(nkc):
                    mm(PS_SC[:, tc * 512:(tc + 1) * 512],
                       AB(slot, nkc * 128)[:, kc * 128:(kc + 1) * 128],
                       srcf(kc, tc), start=(kc == 0), stop=(kc == nkc - 1),
                       r=stags + srcft(kc, tc), w=tgr("PSSC", tc * 512, tc * 512 + 512))
            scr = tgr("PSSC", 0, 2048)
            tA = atag(dof, 2048)
            tB = atag(dof + 2048, 2048)
            if beta is not None:
                vop("dve", "tensor_single_scalar", AB(dof, 2048)[0:64, :], PS_SC[0:64, :],
                    beta[0:64, :], ALU.add, r=scr + stb, w=tA)
                vop("dve", "tensor_single_scalar", AB(dof + 2048, 2048)[64:128, :],
                    PS_SC[64:128, :], beta[64:128, :], ALU.add, r=scr + stb, w=tB)
            else:
                vop("dve", "tensor_copy", AB(dof, 2048)[0:64, :], PS_SC[0:64, :],
                    r=scr, w=tA)
                vop("dve", "tensor_copy", AB(dof + 2048, 2048)[64:128, :],
                    PS_SC[64:128, :], r=scr, w=tB)
            dma("sd", AB(dof, 2048)[64:128, :], AB(dof, 2048)[0:64, :], r=tA, w=tA)
            dma("sd", AB(dof + 2048, 2048)[0:64, :], AB(dof + 2048, 2048)[64:128, :],
                r=tB, w=tB)
        vof = VOF[sset]
        if qk_beta:
            for kc in range(8):
                mm(PS_A[0:1, 128:256], SMB[:, 40 + kc:41 + kc],
                   AB(vslot, 1024)[:, kc * 128:(kc + 1) * 128],
                   start=(kc == 0), stop=(kc == 7), r=tv + ["SMB"], w=["PSA"])
            vop("dve", "tensor_copy", BVC[:], SM[:, 64 + c:65 + c], r=["SM"], w=["BVC"])
            trans(PS_A[:].bitcast(BF16)[0:1, 512:640], BVC[:], r=["BVC", "IDN"],
                  w=["PSA"])
            vop("dve", "tensor_copy", BVT[0:1, :], PS_A[:].bitcast(BF16)[0:1, 512:640],
                r=["PSA"], w=["BVT"])
            vop("dve", "tensor_tensor", BVT[0:1, :], PS_A[0:1, 128:256], BVT[0:1, :],
                ALU.add, r=["PSA", "BVT"], w=["BVT"])
        for tt in range(NT):
            for kc in range(nkc_kv):
                mm(PS_SC[:, tt * 128:(tt + 1) * 128], srckv(kc, None, tt),
                   AB(vslot, nkc_kv * 128)[:, kc * 128:(kc + 1) * 128],
                   start=(kc == 0), stop=(not qk_beta and kc == nkc_kv - 1),
                   r=tv + srckv_t(kc, None, tt),
                   w=tgr("PSSC", tt * 128, tt * 128 + 128))
            if qk_beta:
                mm(PS_SC[:, tt * 128:(tt + 1) * 128], ONES1[:], BVT[0:1, :],
                   start=False, stop=True, r=["ONES1", "BVT"],
                   w=tgr("PSSC", tt * 128, tt * 128 + 128))
        vop("dve", "tensor_copy", AB(vof, 2048), PS_SC[:], r=tgr("PSSC", 0, 2048),
            w=atag(vof, 2048))

    def attention_step(qof, kof, vof, hh, it, njc, diag_r, mask1, psel, evac_fn):
        qtag = atag(qof, 2048)
        ktag = atag(kof, 2048)
        vtag = atag(vof, 2048)
        rot0 = psel % 2
        def R(q):
            return (2 * rot0 + (q % 2)) * 512

        def sc_pair(jc):
            mm(PS_SC[:, R(jc):R(jc) + 512],
               AB(qof, 2048)[0:64, it * 128:(it + 1) * 128],
               AB(kof, 2048)[0:64, jc * 512:(jc + 1) * 512], True, True, tp=(0, 0),
               r=qtag + ktag, w=tgr("PSSC", R(jc), R(jc) + 512))
            if jc + 1 < njc:
                mm(PS_SC[:, R(jc + 1):R(jc + 1) + 512],
                   AB(qof, 2048)[64:128, it * 128:(it + 1) * 128],
                   AB(kof, 2048)[64:128, (jc + 1) * 512:(jc + 2) * 512], True, True,
                   tp=(64, 0), r=qtag + ktag,
                   w=tgr("PSSC", R(jc + 1), R(jc + 1) + 512))
        sc_pair(0)

        def chunk_masks(q):
            if diag_r is not None and q == njc - 1:
                dc = tgr("PSSC", R(q), R(q) + 512)
                vop("dve", "tensor_tensor", PS_SC[:, R(q):R(q) + 512],
                    PS_SC[:, R(q):R(q) + 512],
                    MKC[:, diag_r * 512:(diag_r + 1) * 512], ALU.add, r=dc + ["MKC"], w=dc)
            if mask1 and q == 3:
                dc = tgr("PSSC", R(3), R(3) + 512)
                vop("dve", "tensor_single_scalar", PS_SC[0:1, R(3) + 511:R(3) + 512],
                    PS_SC[0:1, R(3) + 511:R(3) + 512], -999.0, ALU.add, r=dc, w=dc)
        rot = psel % 2
        sta = ["STa%d" % rot]
        nmb = 4 + rot * 4
        nm4 = STA[:, nmb:nmb + njc]
        accb = 12 + rot * 4
        acc4 = STA[:, accb:accb + njc]
        gq = STA[:, 20 + rot:21 + rot]
        f4 = STA[:, 24 + rot * 4:24 + rot * 4 + njc]
        ssum = STA[:, 32 + rot:33 + rot]
        rr = STA[:, 34 + rot:35 + rot]
        psof = PSOF[rot]
        ptags = atag(psof, njc * 512)
        stq = ["STa%d_%d" % (rot, q) for q in range(njc)]
        for q in range(njc):
            if q >= 2 and q % 2 == 0:
                sc_pair(q)
            chunk_masks(q)
            vop("dve", "reduce_max", STA[:, nmb + q:nmb + q + 1],
                PS_SC[:, R(q):R(q) + 512], AX.X, None, None, True,
                r=tgr("PSSC", R(q), R(q) + 512), w=[stq[q]])
        if njc >= 2:
            vop("dve", "tensor_reduce", gq, nm4, AX.X, ALU.min, r=stq, w=sta)
            gbias = gq
        else:
            gbias = STA[:, nmb:nmb + 1]
        for q in range(njc):
            act(AB(psof + q * 512, 512), PS_SC[:, R(q):R(q) + 512], AF.Exp,
                bias=gbias, accum=STA[:, accb + q:accb + q + 1],
                r=tgr("PSSC", R(q), R(q) + 512) + [stq[q]] + sta,
                w=atag(psof + q * 512, 512) + [stq[q]])
        if njc >= 2:
            vop("dve", "reduce_sum", ssum, acc4, AX.X, r=stq + sta, w=sta)
        else:
            vop("dve", "tensor_copy", ssum, acc4, r=stq + sta, w=sta)
        vop("dve", "reciprocal", rr, ssum, r=sta, w=sta)
        njt = it + 1 if diag_r is not None else NT
        psum_att = (PS_A if rot == 0 else PS_B)[:, 0:64]
        attw = ["PSA" if rot == 0 else "PSB"]

        def deferred(psum_att=psum_att, rr=rr, attw=attw, sta=sta, ptags=ptags,
                     psof=psof, njc=njc, njt=njt, hh=hh, vof=vof, vtag=vtag,
                     evac_fn=evac_fn):
            for g0 in range(0, njt, 4):
                n = min(4, njt - g0)
                gs = (g0 // 4) % 2
                for q in range(n):
                    jt = g0 + q
                    trans(TRV[:, gs * 1024 + q * 128: gs * 1024 + q * 128 + 128],
                          AB(psof, njc * 512)[:, jt * 128:(jt + 1) * 128],
                          r=ptags + ["IDN"],
                          w=ptrtag(gs * 1024, gs * 1024 + 1024))
                ptof = PTOF[gs]
                vop("dve", "tensor_copy", AB(ptof, 512)[:, 0:n * 128],
                    TRV[:, gs * 1024:gs * 1024 + n * 128],
                    r=ptrtag(gs * 1024, gs * 1024 + 1024), w=atag(ptof, 512))
                for q in range(n):
                    jt = g0 + q
                    mm(psum_att, AB(ptof, 512)[:, q * 128:(q + 1) * 128],
                       AB(vof, 2048)[:, jt * 128 + hh * 64: jt * 128 + hh * 64 + 64],
                       start=(jt == 0), stop=(jt == njt - 1),
                       r=atag(ptof, 512) + vtag, w=attw)
            evac_fn(psum_att, rr, attw, sta)
        return deferred

    def wo_phase(wo_dr, with_bo, write_hu):
        dma("gd", AB(WOOF, 8192).rearrange("p (kc n) -> p kc n", kc=8),
            wo_dr.rearrange("(kc p) n -> p kc n", p=128), w=atag(WOOF, 8192))
        wo_t = atag(WOOF, 8192)
        for tt in range(NT):
            rot = tt % 2
            yof = PSOF[0] + rot * 2048
            Y1 = AF32(yof, 1024)
            ytags = atag(yof, 1024 * 2)
            C, ctags, rs, t = ln_tile(X[:, tt * 1024:(tt + 1) * 1024],
                                      xtg(tt * 1024, tt * 1024 + 1024),
                                      yof, rot, 1e-5, "STw")
            vop("dve", "tensor_single_scalar", Y1, C, rs, ALU.mult, r=ctags + t, w=ytags)
            for oc in range(2):
                psq = PS_SC[:, (tt % 2) * 1024 + oc * 512: (tt % 2) * 1024 + oc * 512 + 512]
                pst = tgr("PSSC", (tt % 2) * 1024 + oc * 512,
                          (tt % 2) * 1024 + oc * 512 + 512)
                for kc in range(8):
                    mm(psq, ATb[:, kc * 2048 + tt * 128: kc * 2048 + tt * 128 + 128],
                       AB(WOOF, 8192)[:, kc * 1024 + oc * 512: kc * 1024 + oc * 512 + 512],
                       start=(kc == 0), stop=(with_bo is None and kc == 7),
                       r=attag(kc * 2048 + tt * 128, 128) + wo_t, w=pst)
                if with_bo is not None:
                    mm(psq, ONES1[:], with_bo[0:1, oc * 512:(oc + 1) * 512],
                       start=False, stop=True, r=["ONES1", "BOROW"], w=pst)
                hu = AB(HUOF, 16384)[:, tt * 1024 + oc * 512: tt * 1024 + oc * 512 + 512]
                hut = atag(tt * 1024 + oc * 512, 512)
                vop("dve", "tensor_tensor", hu, psq, Y1[:, oc * 512:(oc + 1) * 512],
                    ALU.add, r=pst + ytags, w=hut)
                xr = xtg(tt * 1024 + oc * 512, tt * 1024 + oc * 512 + 512)
                vop("pool", "tensor_tensor",
                    X[:, tt * 1024 + oc * 512: tt * 1024 + oc * 512 + 512],
                    X[:, tt * 1024 + oc * 512: tt * 1024 + oc * 512 + 512], hu, ALU.add,
                    r=hut + xr, w=xr)

    def mlp_halves(w1_slice, w2_slice, gelu, b1, b2row, cbc_src):
        dma("gd", AB(W1OF, 16384).rearrange("p (hc k) -> p hc k", hc=16),
            w1_slice.rearrange("(hc p) k -> p hc k", p=128), w=atag(W1OF, 16384))
        dma("gd", AB(W2OF, 16384).rearrange("p (hc n) -> p hc n", hc=16),
            w2_slice.rearrange("(hc p) n -> p hc n", p=128), w=atag(W2OF, 16384))
        w1t = atag(W1OF, 16384)
        w2t = atag(W2OF, 16384)
        for half in range(2):
            if cbc_src is not None:
                for q in range(2):
                    mm(PS_TR[:, q * 512:(q + 1) * 512], ONES1[:],
                       cbc_src[0:1, half * 1024 + q * 512: half * 1024 + (q + 1) * 512],
                       True, True, r=["STGE", "ONES1"],
                       w=ptrtag(q * 1024, q * 1024 + 1024))
                vop("dve", "tensor_copy", AB(SILOF + 4096, 1024), PS_TR[:, 0:1024],
                    r=ptrtag(0, 2048), w=atag(SILOF + 4096, 1024))
            for hc in range(16):
                psr = PS_SC[:, (hc % 2) * 1024:(hc % 2) * 1024 + 1024]
                pst = tgr("PSSC", (hc % 2) * 1024, (hc % 2) * 1024 + 1024)
                for tc2 in range(2):
                    tc = half * 2 + tc2
                    for kc in range(8):
                        mm(psr[:, tc2 * 512:(tc2 + 1) * 512],
                           AB(W1OF, 16384)[:, hc * 1024 + kc * 128: hc * 1024 + kc * 128 + 128],
                           TT[:, kc * 2048 + tc * 512: kc * 2048 + (tc + 1) * 512],
                           start=(kc == 0), stop=(kc == 7),
                           r=w1t + ttg(kc, tc * 4, tc * 4 + 4), w=pst)
                fn = AF.Gelu_apprx_tanh if gelu else AF.Silu
                bias = SM[:, 72 + hc:73 + hc] if b1 else None
                if cbc_src is None:
                    act(ATb[:, hc * 1024:(hc + 1) * 1024], psr, fn, bias=bias,
                        r=pst + (["SM"] if b1 else []), w=attag(hc * 1024, 1024))
                else:
                    silof = SILOF + 2048 + (hc % 2) * 1024
                    act(AB(silof, 1024), psr, fn, r=pst, w=atag(silof, 1024))
                    vop("pool", "tensor_tensor", ATb[:, hc * 1024:(hc + 1) * 1024],
                        AB(silof, 1024), AB(SILOF + 4096, 1024), ALU.mult,
                        r=atag(silof, 1024) + atag(SILOF + 4096, 1024),
                        w=attag(hc * 1024, 1024))
            for tl in range(8):
                tt = half * 8 + tl
                for oc in range(2):
                    ps = PS_A if (tl * 2 + oc) % 2 == 0 else PS_B
                    pst = ["PSA"] if (tl * 2 + oc) % 2 == 0 else ["PSB"]
                    for hc in range(16):
                        mm(ps[:, 0:512],
                           ATb[:, hc * 1024 + tl * 128: hc * 1024 + tl * 128 + 128],
                           AB(W2OF, 16384)[:, hc * 1024 + oc * 512: hc * 1024 + oc * 512 + 512],
                           start=(hc == 0), stop=(b2row is None and hc == 15),
                           r=attag(hc * 1024 + tl * 128, 128) + w2t, w=pst)
                    if b2row is not None:
                        mm(ps[:, 0:512], ONES1[:], b2row[0:1, oc * 512:(oc + 1) * 512],
                           start=False, stop=True, r=["ONES1", "B2ROW"], w=pst)
                    xr = xtg(tt * 1024 + oc * 512, tt * 1024 + oc * 512 + 512)
                    vop("dve", "tensor_tensor",
                        X[:, tt * 1024 + oc * 512: tt * 1024 + oc * 512 + 512],
                        X[:, tt * 1024 + oc * 512: tt * 1024 + oc * 512 + 512],
                        ps[:, 0:512], ALU.add, r=pst + xr, w=xr)

    def ttq(kc, tc, tt=None):
        if tc is not None:
            return TT[:, kc * 2048 + tc * 512: kc * 2048 + (tc + 1) * 512]
        return TT[:, kc * 2048 + tt * 128: kc * 2048 + tt * 128 + 128]

    def ttq_t(kc, tc, tt=None):
        if tc is not None:
            return ttg(kc, tc * 4, tc * 4 + 4)
        return ttg(kc, tt, tt + 1)

    def sublayer_attn_a():
        norm_to_TT("ln2", lambda tau: X[:, tau * 1024:(tau + 1) * 1024],
                   lambda tau: xtg(tau * 1024, tau * 1024 + 1024), NRMC_A, NRMY_A)
        for c in range(8):
            sset = c % 2
            qkv_pair(c, sset, dr["wq_a"], dr["wk_a"], dr["wv_a"], 8, 8,
                     ttq, ttq_t, ttq, ttq_t, qk_beta=True)
            prev = [None]
            for it in range(NT):
                for hh in range(2):
                    psel = it * 2 + hh

                    def evac(ps_att, rr, attw, sta, c=c, hh=hh, it=it):
                        vop("dve", "tensor_single_scalar", CMB[:, hh * 64:hh * 64 + 64],
                            ps_att, rr, ALU.mult, r=attw + sta, w=["CMB"])
                        if hh == 1:
                            gb = it % 2
                            trans(TRV[:, gb * 1024 + 512: gb * 1024 + 640], CMB[:, 0:128],
                                  r=["CMB", "IDN"], w=ptrtag(gb * 1024, gb * 1024 + 1024))
                            vop("dve", "tensor_copy",
                                ATb[:, c * 2048 + it * 128: c * 2048 + it * 128 + 128],
                                TRV[:, gb * 1024 + 512: gb * 1024 + 640],
                                r=ptrtag(gb * 1024, gb * 1024 + 1024),
                                w=attag(c * 2048 + it * 128, 128))
                    dfr = attention_step(QOF[sset] + (0 if hh == 0 else 2048),
                                         QOF[sset] + 4096 + (0 if hh == 0 else 2048),
                                         VOF[sset], hh, it, 4, None, (it == 0), psel, evac)
                    if prev[0] is not None:
                        prev[0]()
                    prev[0] = dfr
            if prev[0] is not None:
                prev[0]()
                prev[0] = None
        wo_phase(dr["wo_a"], BOROW, True)
        build_bcast(0, SILOF)
        build_bcast(1, SILOF + 1024)
        norm_to_TT("lnaff", lambda tau: AB(HUOF, 16384)[:, tau * 1024:(tau + 1) * 1024],
                   lambda tau: atag(tau * 1024, 1024), NRMC_F, NRMY_F,
                   gbc=(AB(SILOF, 1024), AB(SILOF + 1024, 1024)),
                   gbc_tags=atag(SILOF, 2048))
        mlp_halves(dr["w1_a"], dr["w2_a"], True, True, B2ROW, None)

    def sublayer_ffn():
        build_bcast(2, SILOF)
        norm_to_TT("lnrms", lambda tau: X[:, tau * 1024:(tau + 1) * 1024],
                   lambda tau: xtg(tau * 1024, tau * 1024 + 1024), NRMC_A, NRMY_A,
                   gbc=(AB(SILOF, 1024),), gbc_tags=atag(SILOF, 1024))
        dma("gd", AB(WSOF[0], 4096).rearrange("p (lc k) -> p lc k", lc=4),
            dr["wdkv"].rearrange("(lc p) k -> p lc k", p=128), w=atag(WSOF[0], 4096))
        for lc in range(4):
            for tc in range(4):
                for kc in range(8):
                    mm(PS_SC[:, tc * 512:(tc + 1) * 512],
                       AB(WSOF[0], 4096)[:, lc * 1024 + kc * 128: lc * 1024 + kc * 128 + 128],
                       TT[:, kc * 2048 + tc * 512: kc * 2048 + (tc + 1) * 512],
                       start=(kc == 0), stop=(kc == 7),
                       r=atag(WSOF[0], 4096) + ttg(kc, tc * 4, tc * 4 + 4),
                       w=tgr("PSSC", tc * 512, tc * 512 + 512))
            vop("dve", "tensor_copy", AB(CTOF + lc * 2048, 2048), PS_SC[:],
                r=tgr("PSSC", 0, 2048), w=atag(CTOF + lc * 2048, 2048))

        def ctq(kc, tc, tt=None):
            if tc is not None:
                return AB(CTOF + kc * 2048, 2048)[:, tc * 512:(tc + 1) * 512]
            return AB(CTOF + kc * 2048, 2048)[:, tt * 128:tt * 128 + 128]

        def ctq_t(kc, tc, tt=None):
            return atag(CTOF + kc * 2048, 2048)

        for c in range(8):
            sset = c % 2
            qkv_pair(c, sset, dr["wq_f"], dr["wuk"], dr["wuv"], 8, 4,
                     ttq, ttq_t, ctq, ctq_t, qscale=0.125)
            prev = [None]
            for it in range(NT):
                for hh in range(2):
                    psel = it * 2 + hh

                    def evac(ps_att, rr, attw, sta, c=c, hh=hh, it=it):
                        vop("dve", "tensor_single_scalar", CMB[:, hh * 64:hh * 64 + 64],
                            ps_att, rr, ALU.mult, r=attw + sta, w=["CMB"])
                        if hh == 1:
                            gb = it % 2
                            trans(TRV[:, gb * 1024 + 512: gb * 1024 + 640], CMB[:, 0:128],
                                  r=["CMB", "IDN"], w=ptrtag(gb * 1024, gb * 1024 + 1024))
                            vop("dve", "tensor_copy",
                                ATb[:, c * 2048 + it * 128: c * 2048 + it * 128 + 128],
                                TRV[:, gb * 1024 + 512: gb * 1024 + 640],
                                r=ptrtag(gb * 1024, gb * 1024 + 1024),
                                w=attag(c * 2048 + it * 128, 128))
                    dfr = attention_step(QOF[sset] + (0 if hh == 0 else 2048),
                                         QOF[sset] + 4096 + (0 if hh == 0 else 2048),
                                         VOF[sset], hh, it, it // 4 + 1, it % 4, False,
                                         psel, evac)
                    if prev[0] is not None:
                        prev[0]()
                    prev[0] = dfr
            if prev[0] is not None:
                prev[0]()
                prev[0] = None
        wo_phase(dr["wo_f"], None, True)
        build_bcast(3, SILOF)
        norm_to_TT("rmsg", lambda tau: AB(HUOF, 16384)[:, tau * 1024:(tau + 1) * 1024],
                   lambda tau: atag(tau * 1024, 1024), NRMC_F, NRMY_F,
                   gbc=(AB(SILOF, 1024),), gbc_tags=atag(SILOF, 1024))
        for tt in range(NT):
            for kc in range(8):
                mm(PS_A[:, 0:8], TT[:, kc * 2048 + tt * 128: kc * 2048 + tt * 128 + 128],
                   GWS[:, kc * 8:(kc + 1) * 8], start=(kc == 0), stop=(kc == 7),
                   r=ttg(kc, tt, tt + 1) + ["GWS"], w=["PSA"])
            stg = ["STg"]
            lg = ST[:, 36:44]
            m1 = ST[:, 44:45]
            z8 = ST[:, 45:53]
            i1 = ST[:, 53:61]
            v2 = ST[:, 61:62]
            rD = ST[:, 62:63]
            ez = ST[:, 36:44]
            vop("dve", "tensor_copy", lg, PS_A[:, 0:8], r=["PSA"], w=stg)
            vop("dve", "reduce_max", m1, lg, AX.X, r=stg, w=stg)
            vop("dve", "tensor_single_scalar", z8, lg, m1, ALU.subtract, r=stg, w=stg)
            vop("dve", "tensor_single_scalar", i1, z8, -1e-20, ALU.is_ge, r=stg, w=stg)
            vop("dve", "scalar_tensor_tensor", lg, i1, -1e9, z8, ALU.mult, ALU.add,
                r=stg, w=stg)
            vop("dve", "reduce_max", v2, lg, AX.X, r=stg, w=stg)
            vop("dve", "tensor_single_scalar", lg, lg, v2, ALU.subtract, r=stg, w=stg)
            vop("dve", "tensor_single_scalar", lg, lg, -1e-20, ALU.is_ge, r=stg, w=stg)
            vop("dve", "tensor_tensor", i1, i1, lg, ALU.add, r=stg, w=stg)
            ev2 = ST[:, 63:64]
            act(ez, z8, AF.Exp, r=stg, w=stg)
            act(ev2, v2, AF.Exp, r=stg, w=stg)
            vop("dve", "tensor_single_scalar", ev2, ev2, 1.0, ALU.add, r=stg, w=stg)
            vop("dve", "reciprocal", rD, ev2, r=stg, w=stg)
            vop("dve", "tensor_tensor", ez, ez, i1, ALU.mult, r=stg, w=stg)
            vop("dve", "tensor_single_scalar", CMB[:, 0:8], ez, rD, ALU.mult, r=stg,
                w=["CMB"])
            gb = tt % 2
            trans(TRV[0:8, gb * 1024: gb * 1024 + 128], CMB[:, 0:8], r=["CMB", "IDN"],
                  w=ptrtag(gb * 1024, gb * 1024 + 1024))
            vop("dve", "tensor_copy", CT8[0:8, tt * 128:(tt + 1) * 128],
                TRV[0:8, gb * 1024: gb * 1024 + 128],
                r=ptrtag(gb * 1024, gb * 1024 + 1024), w=["CT8"])
        for e in range(NEXP):
            dma("sd", STGE[0:1, :], CT8[e:e + 1, :], r=["CT8"], w=["STGE"])
            mlp_halves(dr["w1_m"][e * 2048:(e + 1) * 2048, :],
                       dr["w2_m"][e * 2048:(e + 1) * 2048, :],
                       False, False, None, STGE)

    def sublayer_attn_b(qkv_only=False, npairs=8):
        norm_to_TT("ln", lambda tau: X[:, tau * 1024:(tau + 1) * 1024],
                   lambda tau: xtg(tau * 1024, tau * 1024 + 1024), NRMC_A, NRMY_A)
        for c in range(npairs):
            sset = c % 2
            qkv_pair(c, sset, dr["wq_b"], dr["wk_b"], dr["wv_b"], 8, 8,
                     ttq, ttq_t, ttq, ttq_t, qscale=0.125)
            if qkv_only:
                continue
            for hh in range(2):
                qof = QOF[sset] + (0 if hh == 0 else 2048)
                kof = QOF[sset] + 4096 + (0 if hh == 0 else 2048)
                vof = VOF[sset]
                qtags = atag(qof, 2048)
                ktags = atag(kof, 2048)
                vtags = atag(vof, 2048)
                h = 2 * c + hh
                for w0 in range(0, NT, 2):
                    for wi in range(2):
                        w = w0 + wi
                        jlo = max(0, (w - 1) * 128)
                        jhi = min(S, (w + 2) * 128)
                        nj = jhi - jlo
                        pof = wi * 1024
                        rows = slice(0, 64) if wi == 0 else slice(64, 128)
                        mm(PS_SC[:, pof:pof + nj],
                           AB(qof, 2048)[rows, w * 128:(w + 1) * 128],
                           AB(kof, 2048)[rows, jlo:jhi], True, True,
                           tp=(0, 0) if wi == 0 else (64, 0),
                           r=qtags + ktags, w=tgr("PSSC", pof, pof + nj))
                    for wi in range(2):
                        w = w0 + wi
                        jlo = max(0, (w - 1) * 128)
                        jhi = min(S, (w + 2) * 128)
                        nj = jhi - jlo
                        njt = nj // 128
                        pof = wi * 1024
                        scw = tgr("PSSC", pof, pof + nj)
                        sta = ["STa%d" % wi]
                        nm = ST[:, 10 + wi:11 + wi]
                        ssum = ST[:, 20 + wi:21 + wi]
                        rr = ST[:, 22 + wi:23 + wi]
                        vop("dve", "reduce_max", nm,
                            PS_SC[:, pof:pof + nj], AX.X, None, None, True, r=scw, w=sta)
                        psof = PSOF[wi]
                        act(AB(psof, 512)[:, 0:nj], PS_SC[:, pof:pof + nj], AF.Exp,
                            bias=nm, accum=ssum, r=scw + sta, w=atag(psof, 512) + sta)
                        vop("dve", "reciprocal", rr, ssum, r=sta, w=sta)
                        for q in range(njt):
                            trans(TRV[:, wi * 1024 + q * 128: wi * 1024 + (q + 1) * 128],
                                  AB(psof, 512)[:, q * 128:(q + 1) * 128],
                                  r=atag(psof, 512) + ["IDN"],
                                  w=ptrtag(wi * 1024, wi * 1024 + 1024))
                        ptof = PTOF[wi]
                        vop("dve", "tensor_copy",
                            AB(ptof, 512)[:, 0:njt * 128],
                            TRV[:, wi * 1024:wi * 1024 + njt * 128],
                            r=ptrtag(wi * 1024, wi * 1024 + 1024), w=atag(ptof, 512))
                        ps_att = (PS_A if wi == 0 else PS_B)[:, 0:64]
                        attw = ["PSA" if wi == 0 else "PSB"]
                        for q in range(njt):
                            jt = jlo // 128 + q
                            mm(ps_att, AB(ptof, 512)[:, q * 128:(q + 1) * 128],
                               AB(vof, 2048)[:, jt * 128 + hh * 64: jt * 128 + hh * 64 + 64],
                               start=(q == 0), stop=(q == njt - 1),
                               r=atag(ptof, 512) + vtags, w=attw)
                        xr = xtg(w * 1024 + h * 64, w * 1024 + h * 64 + 64)
                        vop("dve", "scalar_tensor_tensor",
                            X[:, w * 1024 + h * 64: w * 1024 + h * 64 + 64],
                            ps_att, rr, X[:, w * 1024 + h * 64: w * 1024 + h * 64 + 64],
                            ALU.mult, ALU.add, r=attw + sta + xr, w=xr)

    if "n1" in PARTS:
        norm_to_TT("ln2", lambda tau: X[:, tau * 1024:(tau + 1) * 1024],
                   lambda tau: xtg(tau * 1024, tau * 1024 + 1024), NRMC_A, NRMY_A)
    if "nl" in PARTS:
        norm_to_TT("ln", lambda tau: X[:, tau * 1024:(tau + 1) * 1024],
                   lambda tau: xtg(tau * 1024, tau * 1024 + 1024), NRMC_A, NRMY_A)
    if any(isinstance(q, str) and q.startswith("nt") for q in PARTS):
        stage = [q for q in PARTS if isinstance(q, str)][0]
        tau = 0
        rot = 0
        Cof = NRMC_A
        yof = NRMY_A
        YH = AB(yof, 1024)
        ytags = atag(yof, 1024)
        C, ctags, rs, t = ln_tile(X[:, 0:1024], xtg(0, 1024), Cof, rot, 1e-5, "STn")
        if stage >= "ntB":
            vop("dve", "tensor_single_scalar", YH, C, rs, ALU.mult, r=ctags + t, w=ytags)
        if stage >= "ntC":
            for c in range(NC):
                g = c % 8
                trw = TRV[:, g * 128:(g + 1) * 128]
                trans(trw, YH[:, c * 128:(c + 1) * 128], r=ytags + ["IDN"],
                      w=ptrtag(g * 128, g * 128 + 128))
                if stage >= "ntD":
                    vop("dve", "tensor_copy",
                        TT[:, c * 2048 + tau * 128: c * 2048 + tau * 128 + 128], trw,
                        r=ptrtag(g * 128, g * 128 + 128), w=["TT%d_%d" % (c, tau)])
    if 1 in PARTS:
        sublayer_attn_a()
    if 2 in PARTS:
        sublayer_ffn()
    if 3 in PARTS:
        sublayer_attn_b()
    if "3q" in PARTS:
        sublayer_attn_b(qkv_only=True, npairs=2)
    if "3a" in PARTS:
        sublayer_attn_b(npairs=1)
    if "3b" in PARTS:
        sublayer_attn_b(npairs=2)
    if "3c" in PARTS:
        sublayer_attn_b(npairs=6)
    if 4 in PARTS:
        sublayer_ffn()

    dma("gd", out, X[:], r=xtg(0, 16384), w=["OUT"])

    final = {s: p.cnt[s] for s in STREAMS}

    def replay(section, eng):
        for st, waits, fn in p.items[section]:
            for s2, q in waits:
                eng.wait_ge(sems[s2], q * INC[s2])
            fn(eng).then_inc(sems[st], INC[st])

    with es:
        with nc.Block() as block:
            @block.sync
            def _(eng):
                replay("sync", eng)
                for s in STREAMS:
                    if final[s]:
                        eng.wait_ge(sems[s], final[s] * INC[s])

            @block.gpsimd
            def _(eng):
                replay("gpsimd", eng)

            @block.vector
            def _(eng):
                replay("vector", eng)

            @block.scalar
            def _(eng):
                replay("scalar", eng)

            @block.tensor
            def _(eng):
                replay("tensor", eng)

    return nc


_CACHE = {}


def kernel(**inputs):
    d = _repack(inputs)
    key = "prog" + str(PARTS)
    if key not in _CACHE:
        _CACHE[key] = build_program()
    nc = _CACHE[key]
    shared = {k: v for k, v in d.items() if k != "x"}
    in_maps = [dict(shared, x=d["x"][b]) for b in range(8)]
    res = run_bass_kernel_spmd(nc, in_maps, core_ids=list(range(8)))
    outs = []
    for b in range(8):
        o = np.asarray(res.results[b]["out"]).reshape(128, NT, D).transpose(1, 0, 2)
        outs.append(o.reshape(S, D))
    return np.stack(outs).astype(np.float32)


# version 04
# speedup vs baseline: 2.1108x; 1.0238x over previous
import sys
import numpy as np
import ml_dtypes

try:
    import concourse.bass as bass
except Exception:
    sys.path.insert(0, "/opt/trn_rl_repo")
    import concourse.bass as bass
import concourse.mybir as mybir
from concourse.bass_utils import run_bass_kernel_spmd

F32 = mybir.dt.float32
BF16 = mybir.dt.bfloat16
AF = mybir.ActivationFunctionType
ALU = mybir.AluOpType
AX = mybir.AxisListType
BFd = ml_dtypes.bfloat16

D = 1024
S = 2048
NT = 16
NC = 8
NEXP = 8

PARTS = (1, 2, 3, 4)

STREAMS = ("pe", "dve", "act", "pool", "sd", "gd")
SECTION = {"pe": "tensor", "dve": "vector", "act": "scalar", "pool": "gpsimd",
           "sd": "sync", "gd": "gpsimd"}
INC = {"pe": 1, "dve": 1, "act": 1, "pool": 1, "sd": 16, "gd": 16}
SAME_ENG_SAFE = 14


class Prog:
    def __init__(self):
        self.items = {"tensor": [], "vector": [], "scalar": [], "gpsimd": [], "sync": []}
        self.cnt = {s: 0 for s in STREAMS}
        self.lastw = {}
        self.lastr = {}
        self.sat = {s: {} for s in STREAMS}

    def emit(self, st, fn, r=(), w=()):
        deps = {}
        hazard = False
        for t in r:
            lw = self.lastw.get(t)
            if lw is not None:
                if deps.get(lw[0], -1) < lw[1]:
                    deps[lw[0]] = lw[1]
                if lw[0] == st and t[0] == "S" and t[1] == "T":
                    hazard = True
        for t in w:
            lw = self.lastw.get(t)
            if lw is not None and deps.get(lw[0], -1) < lw[1]:
                deps[lw[0]] = lw[1]
            for s2, q in self.lastr.get(t, {}).items():
                if deps.get(s2, -1) < q:
                    deps[s2] = q
        self.cnt[st] += 1
        seq = self.cnt[st]
        waits = []
        if st in ("sd", "gd") and seq > 1:
            if self.sat[st].get(st, 0) < seq - 1:
                self.sat[st][st] = seq - 1
                waits.append((st, seq - 1))
        for s2, q in deps.items():
            if s2 == st:
                if (st in ("dve", "pool") and hazard
                        and (seq - 1) - q < SAME_ENG_SAFE):
                    if self.sat[st].get(st, 0) < q:
                        self.sat[st][st] = q
                        waits.append((st, q))
                continue
            if self.sat[st].get(s2, 0) >= q:
                continue
            self.sat[st][s2] = q
            waits.append((s2, q))
        self.items[SECTION[st]].append((st, waits, fn))
        for t in w:
            self.lastw[t] = (st, seq)
            self.lastr[t] = {}
        for t in r:
            d = self.lastr.setdefault(t, {})
            if d.get(st, 0) < seq:
                d[st] = seq
        return seq


def tgr(name, lo, hi, blk=512):
    return [name + str(i) for i in range(lo // blk, (hi + blk - 1) // blk)]


def _repack(inputs):
    f = np.ascontiguousarray
    d = {}
    x = np.asarray(inputs["L0_attn_a_x"], np.float32)
    d["x"] = [f(x[b].reshape(NT, 128, D).transpose(1, 0, 2).reshape(128, NT * D))
              for b in range(8)]

    def colpack(Wm, kb=128):
        K, M = Wm.shape
        nkc = K // kb
        npair = M // 128
        return f(Wm.reshape(nkc, kb, npair, 128).transpose(2, 1, 0, 3)
                 .reshape(npair * kb, nkc * 128))

    d["wq_a"] = colpack(np.asarray(inputs["L0_attn_a_Wq"], np.float32))
    d["wk_a"] = colpack(np.asarray(inputs["L0_attn_a_Wk"], np.float32))
    d["wv_a"] = colpack(np.asarray(inputs["L0_attn_a_Wv"], np.float32))
    d["wo_a"] = f(np.asarray(inputs["L0_attn_a_Wo"], np.float32))
    W1 = np.asarray(inputs["L0_attn_a_W1"], np.float32)
    d["w1_a"] = f(W1.reshape(8, 128, 16, 128).transpose(2, 1, 0, 3).reshape(2048, 1024))
    d["w2_a"] = f(np.asarray(inputs["L0_attn_a_W2"], np.float32))

    d["wq_f"] = colpack(np.asarray(inputs["L0_ffn_Wq"], np.float32))
    Wdkv = np.asarray(inputs["L0_ffn_Wdkv"], np.float32)
    d["wdkv"] = f(Wdkv.reshape(8, 128, 4, 128).transpose(2, 1, 0, 3).reshape(512, 1024))
    Wuk = np.asarray(inputs["L0_ffn_Wuk"], np.float32)
    d["wuk"] = f(Wuk.reshape(4, 128, 8, 128).transpose(2, 1, 0, 3).reshape(1024, 512))
    Wuv = np.asarray(inputs["L0_ffn_Wuv"], np.float32)
    d["wuv"] = f(Wuv.reshape(4, 128, 8, 128).transpose(2, 1, 0, 3).reshape(1024, 512))
    d["wo_f"] = f(np.asarray(inputs["L0_ffn_Wo"], np.float32))

    wqkv = np.asarray(inputs["L0_attn_b_w_qkv"], np.float32)
    d["wq_b"] = colpack(wqkv[:, 0:1024])
    d["wk_b"] = colpack(wqkv[:, 1024:2048])
    d["wv_b"] = colpack(wqkv[:, 2048:3072])

    w1m = np.asarray(inputs["L0_ffn_w1"], np.float32)
    d["w1_m"] = f(w1m.reshape(NEXP, 8, 128, 16, 128).transpose(0, 3, 2, 1, 4)
                  .reshape(NEXP * 2048, 1024))
    d["w2_m"] = f(np.asarray(inputs["L0_ffn_w2"], np.float32).reshape(NEXP * 2048, D))

    d["gws"] = f(np.asarray(inputs["L0_ffn_gate_w"], np.float32)
                 .reshape(8, 128, 8).transpose(1, 0, 2).reshape(128, 64))

    def pcol(v):
        return np.asarray(v, np.float32).reshape(8, 128).T

    sm = np.zeros((128, 96), np.float32)
    sm[:, 0:8] = pcol(inputs["L0_attn_a_lnq_s"]); sm[:, 8:16] = pcol(inputs["L0_attn_a_lnq_b"])
    sm[:, 16:24] = pcol(inputs["L0_attn_a_lnk_s"]); sm[:, 24:32] = pcol(inputs["L0_attn_a_lnk_b"])
    sm[:, 32:40] = pcol(inputs["L0_attn_a_lnv_s"]); sm[:, 40:48] = pcol(inputs["L0_attn_a_lnv_b"])
    sm[:, 48:56] = pcol(inputs["L0_attn_a_bq"]); sm[:, 56:64] = pcol(inputs["L0_attn_a_bk"])
    sm[:, 64:72] = pcol(inputs["L0_attn_a_bv"])
    sm[:, 72:88] = np.asarray(inputs["L0_attn_a_b1"], np.float32).reshape(16, 128).T
    d["smalls"] = f(sm)

    rows = np.zeros((8, 1024), np.float32)
    rows[0] = np.asarray(inputs["L0_attn_a_lnm_s"], np.float32)
    rows[1] = np.asarray(inputs["L0_attn_a_lnm_b"], np.float32)
    rows[2] = np.asarray(inputs["L0_ffn_g1"], np.float32)
    rows[3] = np.asarray(inputs["L0_ffn_g2"], np.float32)
    rows[4] = np.asarray(inputs["L0_attn_a_bo"], np.float32)
    rows[5] = np.asarray(inputs["L0_attn_a_b2"], np.float32)
    d["rows"] = f(rows)

    d["idn"] = np.eye(128, dtype=np.float32).astype(BFd)
    mk = np.zeros((128, 4, 512), np.float32)
    ii = np.arange(128)[:, None]
    jj = np.arange(512)[None, :]
    for rr in range(4):
        mk[:, rr, :] = np.where(jj <= rr * 128 + ii, 0.0, -30000.0)
    d["maskc"] = f(mk.reshape(128, 2048)).astype(BFd)
    return d


def build_program():
    nc = bass.Bass("TRN2", target_bir_lowering=False)
    dr = {}
    dr["x"] = nc.dram_tensor("x", [128, NT * D], F32, kind="ExternalInput").ap()
    for n, shape in [("wq_a", [1024, 1024]), ("wk_a", [1024, 1024]), ("wv_a", [1024, 1024]),
                     ("wo_a", [1024, 1024]), ("w1_a", [2048, 1024]), ("w2_a", [2048, 1024]),
                     ("wq_f", [1024, 1024]), ("wdkv", [512, 1024]), ("wuk", [1024, 512]),
                     ("wuv", [1024, 512]), ("wo_f", [1024, 1024]),
                     ("wq_b", [1024, 1024]), ("wk_b", [1024, 1024]), ("wv_b", [1024, 1024]),
                     ("w1_m", [NEXP * 2048, 1024]), ("w2_m", [NEXP * 2048, 1024]),
                     ("gws", [128, 64]), ("smalls", [128, 96]), ("rows", [8, 1024])]:
        dr[n] = nc.dram_tensor(n, shape, F32, kind="ExternalInput").ap()
    dr["idn"] = nc.dram_tensor("idn", [128, 128], BF16, kind="ExternalInput").ap()
    dr["maskc"] = nc.dram_tensor("maskc", [128, 2048], BF16, kind="ExternalInput").ap()
    out = nc.dram_tensor("out", [128, NT * D], F32, kind="ExternalOutput").ap()

    from contextlib import ExitStack
    es = ExitStack()
    X = es.enter_context(nc.sbuf_tensor("X", [128, 16384], BF16))
    TT = es.enter_context(nc.sbuf_tensor("TT", [128, 16384], BF16))
    ATb = es.enter_context(nc.sbuf_tensor("ATb", [128, 16384], BF16))
    AR = es.enter_context(nc.sbuf_tensor("AR", [128, 40960], BF16))
    IDN = es.enter_context(nc.sbuf_tensor("IDN", [128, 128], BF16))
    MKC = es.enter_context(nc.sbuf_tensor("MKC", [128, 2048], BF16))
    SM = es.enter_context(nc.sbuf_tensor("SM", [128, 96], F32))
    SMB = es.enter_context(nc.sbuf_tensor("SMB", [128, 96], BF16))
    GWS = es.enter_context(nc.sbuf_tensor("GWS", [128, 64], BF16))
    ST = es.enter_context(nc.sbuf_tensor("ST", [128, 64], F32))
    STA = es.enter_context(nc.sbuf_tensor("STA", [128, 40], F32))
    ONES1 = es.enter_context(nc.sbuf_tensor("ONES1", [1, 128], BF16))
    EPS = es.enter_context(nc.sbuf_tensor("EPS", [128, 2], F32))
    BOROW = es.enter_context(nc.sbuf_tensor("BOROW", [1, 1024], BF16))
    B2ROW = es.enter_context(nc.sbuf_tensor("B2ROW", [1, 1024], BF16))
    RSTG = es.enter_context(nc.sbuf_tensor("RSTG", [1, 1024], BF16))
    STGE = es.enter_context(nc.sbuf_tensor("STGE", [1, 2048], BF16))
    BVT = es.enter_context(nc.sbuf_tensor("BVT", [1, 128], BF16))
    BVC = es.enter_context(nc.sbuf_tensor("BVC", [128, 1], BF16))
    CT8 = es.enter_context(nc.sbuf_tensor("CT8", [8, 2048], BF16))
    CMB = es.enter_context(nc.sbuf_tensor("CMB", [128, 128], BF16))
    PS_SC = es.enter_context(nc.psum_tensor("PS_SC", [128, 2048], F32))
    PS_TR = es.enter_context(nc.psum_tensor("PS_TR", [128, 1024], F32))
    PS_A = es.enter_context(nc.psum_tensor("PS_A", [128, 512], F32))
    PS_B = es.enter_context(nc.psum_tensor("PS_B", [128, 512], F32))
    sems = {s: es.enter_context(nc.semaphore("sem_" + s)) for s in STREAMS}

    p = Prog()

    def AB(off, n):
        return AR[:, off:off + n]

    def AF32(offb, nf):
        return AR[:].bitcast(F32)[:, offb // 2: offb // 2 + nf]

    def atag(off_bf, n_bf):
        return tgr("AR", off_bf, off_bf + n_bf)

    def attag(off, n):
        return tgr("AT", off, off + n, 1024)

    def xtg(off, n):
        return tgr("X", off, off + n)

    def ttg(kc, tlo, thi):
        return ["TT%d_%d" % (kc, t) for t in range(tlo, thi)]

    def ptrtag(bflo, bfhi):
        return tgr("PTRB", bflo, bfhi, 1024)

    QOF = [0, 8192]
    VOF = [16384, 18432]
    PSOF = [20480, 22528]
    PTOF = [24576, 25088]
    WSOF = [25728, 28800]
    CTOF = 32768
    WOOF = 32768
    W2OF = 0
    W1OF = 16384
    SILOF = 32768
    HUOF = 0
    NRMC_A = 0
    NRMY_A = 8192
    NRMC_F = 24576
    NRMY_F = 20480

    def dma(st, o, i, r=(), w=()):
        p.emit(st, (lambda o=o, i=i: (lambda e: e.dma_start(out=o, in_=i)))(), r=r, w=w)

    def mm(psum, lhsT, rhs, start, stop, tp=None, r=(), w=()):
        p.emit("pe", (lambda psum=psum, lhsT=lhsT, rhs=rhs, start=start, stop=stop, tp=tp:
                      (lambda e: e.matmul(psum, lhsT, rhs, start=start, stop=stop,
                                          tile_position=tp)))(), r=r, w=w)

    def trans(psum, inp, r=(), w=()):
        p.emit("pe", (lambda psum=psum, inp=inp:
                      (lambda e: e.transpose(psum, inp, IDN[:])))(), r=r, w=w)

    def act(o, i, func, bias=None, scale=1.0, accum=None, r=(), w=()):
        kw = {}
        if bias is not None:
            kw["bias"] = bias
        if accum is not None:
            kw["accum_out"] = accum
        p.emit("act", (lambda o=o, i=i, func=func, scale=scale, kw=kw:
                       (lambda e: e.activation(o, i, func, scale=scale, **kw)))(), r=r, w=w)

    def vop(st, meth, *args, r=(), w=()):
        p.emit(st, (lambda meth=meth, args=args:
                    (lambda e: getattr(e, meth)(*args)))(), r=r, w=w)

    TRV = PS_TR[:].bitcast(BF16)

    dma("gd", X[:], dr["x"], w=xtg(0, 16384))
    dma("sd", IDN[:], dr["idn"], w=["IDN"])
    dma("sd", MKC[:], dr["maskc"], w=["MKC"])
    dma("sd", SM[:], dr["smalls"], w=["SM"])
    dma("gd", GWS[:], dr["gws"], w=["GWS"])
    dma("gd", BOROW[:], dr["rows"][4:5, :], w=["BOROW"])
    dma("gd", B2ROW[:], dr["rows"][5:6, :], w=["B2ROW"])
    vop("pool", "memset", ONES1[:], 1.0, w=["ONES1"])
    vop("pool", "tensor_copy", SMB[:], SM[:], r=["SM"], w=["SMB"])
    vop("pool", "memset", EPS[:, 0:1], 1e-5, w=["EPS"])
    vop("pool", "memset", EPS[:, 1:2], 1e-6, w=["EPS"])

    def ln_tile(src, stags, Cof, rot, eps, stn):
        C = AF32(Cof, 1024)
        ctags = atag(Cof, 2048)
        C2of = Cof + 4096
        C2 = AF32(C2of, 1024)
        c2tags = atag(C2of, 2048)
        mu = ST[:, 0 + rot:1 + rot] if stn == "STn" else ST[:, 30 + rot:31 + rot]
        vv = ST[:, 2 + rot:3 + rot] if stn == "STn" else ST[:, 32 + rot:33 + rot]
        rs = ST[:, 4 + rot:5 + rot] if stn == "STn" else ST[:, 34 + rot:35 + rot]
        t = [stn + str(rot)]
        vop("dve", "reduce_sum", mu, src, AX.X, r=stags, w=t)
        vop("dve", "tensor_single_scalar", mu, mu, 1.0 / D, ALU.mult, r=t, w=t)
        vop("dve", "scalar_tensor_tensor", C, src, mu, src, ALU.subtract, ALU.bypass,
            r=stags + t, w=ctags)
        act(C2, C, AF.Square, accum=vv, r=ctags + t, w=c2tags + t)
        vop("dve", "tensor_single_scalar", vv, vv, 1.0 / D, ALU.mult, r=t, w=t)
        act(rs, vv, AF.Sqrt, bias=EPS[:, 0:1] if eps == 1e-5 else EPS[:, 1:2],
            r=t + ["EPS"], w=t)
        vop("dve", "reciprocal", rs, rs, r=t, w=t)
        return C, ctags, rs, t

    def rms_tile(src, stags, Cof, rot, eps, stn="STn"):
        C2of = Cof + 4096
        C2 = AF32(C2of, 1024)
        vv = ST[:, 2 + rot:3 + rot]
        rs = ST[:, 4 + rot:5 + rot]
        t = [stn + str(rot)]
        act(C2, src, AF.Square, accum=vv, r=stags + t, w=atag(C2of, 2048) + t)
        vop("dve", "tensor_single_scalar", vv, vv, 1.0 / D, ALU.mult, r=t, w=t)
        act(rs, vv, AF.Sqrt, bias=EPS[:, 0:1] if eps == 1e-5 else EPS[:, 1:2],
            r=t + ["EPS"], w=t)
        vop("dve", "reciprocal", rs, rs, r=t, w=t)
        return rs, t

    def transpose_to_TT(YH, ytags, tau, dst=None):
        dstt = TT if dst is None else dst
        t3 = dstt[:].rearrange("p (c m) -> p c m", c=8)
        for grp in range(2):
            bank = (tau * 2 + grp) % 2
            for q in range(4):
                c = grp * 4 + q
                trw = TRV[:, bank * 1024 + q * 128: bank * 1024 + (q + 1) * 128]
                trans(trw, YH[:, c * 128:(c + 1) * 128], r=ytags + ["IDN"],
                      w=ptrtag(bank * 1024, bank * 1024 + 1024))
            vop("dve", "tensor_copy",
                t3[:, grp * 4:grp * 4 + 4, tau * 128: tau * 128 + 128],
                TRV[:, bank * 1024: bank * 1024 + 512].rearrange("p (c m) -> p c m", c=4),
                r=ptrtag(bank * 1024, bank * 1024 + 1024),
                w=["TT%d_%d" % (grp * 4 + q, tau) for q in range(4)])

    def norm_to_TT(kind, src_fn, stags_fn, cbase, ybase, gbc=None, gbc_tags=(), eps=1e-5):
        for tau in range(NT):
            rot = tau % 2
            Cof = cbase + rot * 2048
            yof = ybase + rot * 2048
            YH = AB(yof, 1024)
            ytags = atag(yof, 1024)
            src = src_fn(tau)
            stags = stags_fn(tau)
            if kind == "ln":
                C, ctags, rs, t = ln_tile(src, stags, Cof, rot, eps, "STn")
                vop("dve", "tensor_single_scalar", YH, C, rs, ALU.mult, r=ctags + t, w=ytags)
            elif kind == "ln2":
                C, ctags, rs, t = ln_tile(src, stags, Cof, rot, eps, "STn")
                vop("dve", "tensor_single_scalar", C, C, rs, ALU.mult, r=ctags + t, w=ctags)
                C, ctags, rs, t = ln_tile(C, ctags, Cof, rot, eps, "STn")
                vop("dve", "tensor_single_scalar", YH, C, rs, ALU.mult, r=ctags + t, w=ytags)
            elif kind == "lnaff":
                C, ctags, rs, t = ln_tile(src, stags, Cof, rot, eps, "STn")
                vop("dve", "tensor_single_scalar", C, C, rs, ALU.mult, r=ctags + t, w=ctags)
                vop("pool", "tensor_tensor", C, C, gbc[0], ALU.mult,
                    r=ctags + list(gbc_tags), w=ctags)
                vop("pool", "tensor_tensor", YH, C, gbc[1], ALU.add,
                    r=ctags + list(gbc_tags), w=ytags)
            elif kind == "lnrms":
                C, ctags, rs, t = ln_tile(src, stags, Cof, rot, 1e-5, "STn")
                vop("dve", "tensor_single_scalar", C, C, rs, ALU.mult, r=ctags + t, w=ctags)
                rs2, t2 = rms_tile(C, ctags, Cof, rot, 1e-6)
                vop("dve", "tensor_single_scalar", C, C, rs2, ALU.mult, r=ctags + t2, w=ctags)
                vop("pool", "tensor_tensor", YH, C, gbc[0], ALU.mult,
                    r=ctags + list(gbc_tags), w=ytags)
            elif kind == "rmsg":
                rs2, t2 = rms_tile(src, stags, Cof, rot, 1e-6)
                vop("dve", "tensor_single_scalar", AF32(Cof, 1024), src, rs2, ALU.mult,
                    r=stags + t2, w=atag(Cof, 2048))
                vop("pool", "tensor_tensor", YH, AF32(Cof, 1024), gbc[0], ALU.mult,
                    r=atag(Cof, 2048) + list(gbc_tags), w=ytags)
            transpose_to_TT(YH, ytags, tau)

    def build_bcast(row_idx, dstof):
        dma("gd", RSTG[0:1, :], dr["rows"][row_idx:row_idx + 1, :], r=["RSTG"], w=["RSTG"])
        for q in range(2):
            mm(PS_TR[:, q * 512:(q + 1) * 512], ONES1[:],
               RSTG[0:1, q * 512:(q + 1) * 512], True, True,
               r=["RSTG", "ONES1"], w=ptrtag(q * 1024, q * 1024 + 1024))
        vop("dve", "tensor_copy", AB(dstof, 1024), PS_TR[:, 0:1024],
            r=ptrtag(0, 2048), w=atag(dstof, 1024))

    def qkv_pair(c, sset, wq_dr, wk_dr, wv_dr, nkc_q, nkc_kv, srcq, srcq_t, srckv, srckv_t,
                 qk_beta=False, qscale=None):
        wof = WSOF[sset]
        qslot, kslot, vslot = wof, wof + 1024, wof + 2048
        tq = atag(qslot, 1024)
        tk = atag(kslot, 1024)
        tv = atag(vslot, 1024)
        dma("gd", AB(qslot, nkc_q * 128), wq_dr[c * 128:(c + 1) * 128, :], w=tq)
        dma("gd", AB(kslot, nkc_kv * 128), wk_dr[c * 128:(c + 1) * 128, :], w=tk)
        dma("gd", AB(vslot, nkc_kv * 128), wv_dr[c * 128:(c + 1) * 128, :], w=tv)
        rot = c % 2
        bq = ST[:, 24 + rot:25 + rot]
        bk = ST[:, 26 + rot:27 + rot]
        bv = ST[:, 28 + rot:29 + rot]
        stb = ["STb%d" % rot]
        if qk_beta:
            for (slot, stags, lns, lnb, bcol, beta, extra) in (
                    (qslot, tq, 0, 8, 48, bq, 1.0 / 32.0),
                    (kslot, tk, 16, 24, 56, bk, None),
                    (vslot, tv, 32, 40, 64, bv, None)):
                for kc in range(8):
                    mm(PS_A[:, 64:65], AB(slot, 1024)[:, kc * 128:(kc + 1) * 128],
                       SMB[:, lnb + kc:lnb + kc + 1], start=(kc == 0), stop=(kc == 7),
                       r=stags + ["SMB"], w=["PSA"])
                vop("dve", "tensor_tensor", beta, PS_A[:, 64:65],
                    SM[:, bcol + c:bcol + c + 1], ALU.add, r=["PSA", "SM"], w=stb)
                if extra is not None:
                    vop("dve", "tensor_single_scalar", beta, beta, extra, ALU.mult,
                        r=stb, w=stb)
                for kc in range(8):
                    sc = SM[:, lns + kc:lns + kc + 1]
                    vop("dve", "tensor_single_scalar",
                        AB(slot, 1024)[:, kc * 128:(kc + 1) * 128],
                        AB(slot, 1024)[:, kc * 128:(kc + 1) * 128], sc, ALU.mult,
                        r=stags + ["SM"], w=stags)
                    if extra is not None:
                        vop("dve", "tensor_single_scalar",
                            AB(slot, 1024)[:, kc * 128:(kc + 1) * 128],
                            AB(slot, 1024)[:, kc * 128:(kc + 1) * 128], extra, ALU.mult,
                            r=stags, w=stags)
        elif qscale is not None:
            vop("dve", "tensor_single_scalar", AB(qslot, nkc_q * 128),
                AB(qslot, nkc_q * 128), qscale, ALU.mult, r=tq, w=tq)
        qof = QOF[sset]
        for (slot, stags, dof, nkc, srcf, srcft, beta) in (
                (qslot, tq, qof, nkc_q, srcq, srcq_t, bq if qk_beta else None),
                (kslot, tk, qof + 4096, nkc_kv, srckv, srckv_t, bk if qk_beta else None)):
            for tc in range(4):
                for kc in range(nkc):
                    mm(PS_SC[:, tc * 512:(tc + 1) * 512],
                       AB(slot, nkc * 128)[:, kc * 128:(kc + 1) * 128],
                       srcf(kc, tc), start=(kc == 0), stop=(kc == nkc - 1),
                       r=stags + srcft(kc, tc), w=tgr("PSSC", tc * 512, tc * 512 + 512))
            scr = tgr("PSSC", 0, 2048)
            tA = atag(dof, 2048)
            tB = atag(dof + 2048, 2048)
            if beta is not None:
                vop("dve", "tensor_single_scalar", AB(dof, 2048)[0:64, :], PS_SC[0:64, :],
                    beta[0:64, :], ALU.add, r=scr + stb, w=tA)
                vop("dve", "tensor_single_scalar", AB(dof + 2048, 2048)[64:128, :],
                    PS_SC[64:128, :], beta[64:128, :], ALU.add, r=scr + stb, w=tB)
            else:
                vop("dve", "tensor_copy", AB(dof, 2048)[0:64, :], PS_SC[0:64, :],
                    r=scr, w=tA)
                vop("dve", "tensor_copy", AB(dof + 2048, 2048)[64:128, :],
                    PS_SC[64:128, :], r=scr, w=tB)
            dma("sd", AB(dof, 2048)[64:128, :], AB(dof, 2048)[0:64, :], r=tA, w=tA)
            dma("sd", AB(dof + 2048, 2048)[0:64, :], AB(dof + 2048, 2048)[64:128, :],
                r=tB, w=tB)
        vof = VOF[sset]
        if qk_beta:
            for kc in range(8):
                mm(PS_A[0:1, 128:256], SMB[:, 40 + kc:41 + kc],
                   AB(vslot, 1024)[:, kc * 128:(kc + 1) * 128],
                   start=(kc == 0), stop=(kc == 7), r=tv + ["SMB"], w=["PSA"])
            vop("dve", "tensor_copy", BVC[:], SM[:, 64 + c:65 + c], r=["SM"], w=["BVC"])
            trans(PS_A[:].bitcast(BF16)[0:1, 512:640], BVC[:], r=["BVC", "IDN"],
                  w=["PSA"])
            vop("dve", "tensor_copy", BVT[0:1, :], PS_A[:].bitcast(BF16)[0:1, 512:640],
                r=["PSA"], w=["BVT"])
            vop("dve", "tensor_tensor", BVT[0:1, :], PS_A[0:1, 128:256], BVT[0:1, :],
                ALU.add, r=["PSA", "BVT"], w=["BVT"])
        for tt in range(NT):
            for kc in range(nkc_kv):
                mm(PS_SC[:, tt * 128:(tt + 1) * 128], srckv(kc, None, tt),
                   AB(vslot, nkc_kv * 128)[:, kc * 128:(kc + 1) * 128],
                   start=(kc == 0), stop=(not qk_beta and kc == nkc_kv - 1),
                   r=tv + srckv_t(kc, None, tt),
                   w=tgr("PSSC", tt * 128, tt * 128 + 128))
            if qk_beta:
                mm(PS_SC[:, tt * 128:(tt + 1) * 128], ONES1[:], BVT[0:1, :],
                   start=False, stop=True, r=["ONES1", "BVT"],
                   w=tgr("PSSC", tt * 128, tt * 128 + 128))
        vop("dve", "tensor_copy", AB(vof, 2048), PS_SC[:], r=tgr("PSSC", 0, 2048),
            w=atag(vof, 2048))

    def attention_step(qof, kof, vof, hh, it, njc, diag_r, mask1, psel, evac_fn):
        qtag = atag(qof, 2048)
        ktag = atag(kof, 2048)
        vtag = atag(vof, 2048)
        rot0 = psel % 2
        def R(q):
            return (2 * rot0 + (q % 2)) * 512

        def sc_pair(jc):
            mm(PS_SC[:, R(jc):R(jc) + 512],
               AB(qof, 2048)[0:64, it * 128:(it + 1) * 128],
               AB(kof, 2048)[0:64, jc * 512:(jc + 1) * 512], True, True, tp=(0, 0),
               r=qtag + ktag, w=tgr("PSSC", R(jc), R(jc) + 512))
            if jc + 1 < njc:
                mm(PS_SC[:, R(jc + 1):R(jc + 1) + 512],
                   AB(qof, 2048)[64:128, it * 128:(it + 1) * 128],
                   AB(kof, 2048)[64:128, (jc + 1) * 512:(jc + 2) * 512], True, True,
                   tp=(64, 0), r=qtag + ktag,
                   w=tgr("PSSC", R(jc + 1), R(jc + 1) + 512))
        sc_pair(0)

        def chunk_masks(q):
            if diag_r is not None and q == njc - 1:
                dc = tgr("PSSC", R(q), R(q) + 512)
                vop("dve", "tensor_tensor", PS_SC[:, R(q):R(q) + 512],
                    PS_SC[:, R(q):R(q) + 512],
                    MKC[:, diag_r * 512:(diag_r + 1) * 512], ALU.add, r=dc + ["MKC"], w=dc)
            if mask1 and q == 3:
                dc = tgr("PSSC", R(3), R(3) + 512)
                vop("dve", "tensor_single_scalar", PS_SC[0:1, R(3) + 511:R(3) + 512],
                    PS_SC[0:1, R(3) + 511:R(3) + 512], -999.0, ALU.add, r=dc, w=dc)
        rot = psel % 2
        sta = ["STa%d" % rot]
        nmb = 4 + rot * 4
        nm4 = STA[:, nmb:nmb + njc]
        accb = 12 + rot * 4
        acc4 = STA[:, accb:accb + njc]
        gq = STA[:, 20 + rot:21 + rot]
        f4 = STA[:, 24 + rot * 4:24 + rot * 4 + njc]
        ssum = STA[:, 32 + rot:33 + rot]
        rr = STA[:, 34 + rot:35 + rot]
        psof = PSOF[rot]
        ptags = atag(psof, njc * 512)
        stq = ["STa%d_%d" % (rot, q) for q in range(njc)]
        for q in range(njc):
            if q >= 2 and q % 2 == 0:
                sc_pair(q)
            chunk_masks(q)
            vop("dve", "reduce_max", STA[:, nmb + q:nmb + q + 1],
                PS_SC[:, R(q):R(q) + 512], AX.X, None, None, True,
                r=tgr("PSSC", R(q), R(q) + 512), w=[stq[q]])
        if njc >= 2:
            vop("dve", "tensor_reduce", gq, nm4, AX.X, ALU.min, r=stq, w=sta)
            gbias = gq
        else:
            gbias = STA[:, nmb:nmb + 1]
        for q in range(njc):
            act(AB(psof + q * 512, 512), PS_SC[:, R(q):R(q) + 512], AF.Exp,
                bias=gbias, accum=STA[:, accb + q:accb + q + 1],
                r=tgr("PSSC", R(q), R(q) + 512) + [stq[q]] + sta,
                w=atag(psof + q * 512, 512) + [stq[q]])
        if njc >= 2:
            vop("dve", "reduce_sum", ssum, acc4, AX.X, r=stq + sta, w=sta)
        else:
            vop("dve", "tensor_copy", ssum, acc4, r=stq + sta, w=sta)
        vop("dve", "reciprocal", rr, ssum, r=sta, w=sta)
        njt = it + 1 if diag_r is not None else NT
        psum_att = (PS_A if rot == 0 else PS_B)[:, 0:64]
        attw = ["PSA" if rot == 0 else "PSB"]

        def deferred(psum_att=psum_att, rr=rr, attw=attw, sta=sta, ptags=ptags,
                     psof=psof, njc=njc, njt=njt, hh=hh, vof=vof, vtag=vtag,
                     evac_fn=evac_fn):
            for g0 in range(0, njt, 4):
                n = min(4, njt - g0)
                gs = (g0 // 4) % 2
                for q in range(n):
                    jt = g0 + q
                    trans(TRV[:, gs * 1024 + q * 128: gs * 1024 + q * 128 + 128],
                          AB(psof, njc * 512)[:, jt * 128:(jt + 1) * 128],
                          r=ptags + ["IDN"],
                          w=ptrtag(gs * 1024, gs * 1024 + 1024))
                ptof = PTOF[gs]
                vop("dve", "tensor_copy", AB(ptof, 512)[:, 0:n * 128],
                    TRV[:, gs * 1024:gs * 1024 + n * 128],
                    r=ptrtag(gs * 1024, gs * 1024 + 1024), w=atag(ptof, 512))
                for q in range(n):
                    jt = g0 + q
                    mm(psum_att, AB(ptof, 512)[:, q * 128:(q + 1) * 128],
                       AB(vof, 2048)[:, jt * 128 + hh * 64: jt * 128 + hh * 64 + 64],
                       start=(jt == 0), stop=(jt == njt - 1),
                       r=atag(ptof, 512) + vtag, w=attw)
            evac_fn(psum_att, rr, attw, sta)
        return deferred

    def wo_phase(wo_dr, with_bo, write_hu):
        dma("gd", AB(WOOF, 8192).rearrange("p (kc n) -> p kc n", kc=8),
            wo_dr.rearrange("(kc p) n -> p kc n", p=128), w=atag(WOOF, 8192))
        wo_t = atag(WOOF, 8192)
        for tt in range(NT):
            rot = tt % 2
            yof = PSOF[0] + rot * 2048
            Y1 = AF32(yof, 1024)
            ytags = atag(yof, 1024 * 2)
            C, ctags, rs, t = ln_tile(X[:, tt * 1024:(tt + 1) * 1024],
                                      xtg(tt * 1024, tt * 1024 + 1024),
                                      yof, rot, 1e-5, "STw")
            vop("dve", "tensor_single_scalar", Y1, C, rs, ALU.mult, r=ctags + t, w=ytags)
            for oc in range(2):
                psq = PS_SC[:, (tt % 2) * 1024 + oc * 512: (tt % 2) * 1024 + oc * 512 + 512]
                pst = tgr("PSSC", (tt % 2) * 1024 + oc * 512,
                          (tt % 2) * 1024 + oc * 512 + 512)
                for kc in range(8):
                    mm(psq, ATb[:, kc * 2048 + tt * 128: kc * 2048 + tt * 128 + 128],
                       AB(WOOF, 8192)[:, kc * 1024 + oc * 512: kc * 1024 + oc * 512 + 512],
                       start=(kc == 0), stop=(with_bo is None and kc == 7),
                       r=attag(kc * 2048 + tt * 128, 128) + wo_t, w=pst)
                if with_bo is not None:
                    mm(psq, ONES1[:], with_bo[0:1, oc * 512:(oc + 1) * 512],
                       start=False, stop=True, r=["ONES1", "BOROW"], w=pst)
                hu = AB(HUOF, 16384)[:, tt * 1024 + oc * 512: tt * 1024 + oc * 512 + 512]
                hut = atag(tt * 1024 + oc * 512, 512)
                vop("dve", "tensor_tensor", hu, psq, Y1[:, oc * 512:(oc + 1) * 512],
                    ALU.add, r=pst + ytags, w=hut)
                xr = xtg(tt * 1024 + oc * 512, tt * 1024 + oc * 512 + 512)
                vop("pool", "tensor_tensor",
                    X[:, tt * 1024 + oc * 512: tt * 1024 + oc * 512 + 512],
                    X[:, tt * 1024 + oc * 512: tt * 1024 + oc * 512 + 512], hu, ALU.add,
                    r=hut + xr, w=xr)

    def mlp_halves(w1_slice, w2_slice, gelu, b1, b2row, cbc_src):
        dma("gd", AB(W1OF, 16384).rearrange("p (hc k) -> p hc k", hc=16),
            w1_slice.rearrange("(hc p) k -> p hc k", p=128), w=atag(W1OF, 16384))
        dma("gd", AB(W2OF, 16384).rearrange("p (hc n) -> p hc n", hc=16),
            w2_slice.rearrange("(hc p) n -> p hc n", p=128), w=atag(W2OF, 16384))
        w1t = atag(W1OF, 16384)
        w2t = atag(W2OF, 16384)
        for half in range(2):
            if cbc_src is not None:
                for q in range(2):
                    mm(PS_TR[:, q * 512:(q + 1) * 512], ONES1[:],
                       cbc_src[0:1, half * 1024 + q * 512: half * 1024 + (q + 1) * 512],
                       True, True, r=["STGE", "ONES1"],
                       w=ptrtag(q * 1024, q * 1024 + 1024))
                vop("dve", "tensor_copy", AB(SILOF + 4096, 1024), PS_TR[:, 0:1024],
                    r=ptrtag(0, 2048), w=atag(SILOF + 4096, 1024))
            for hc in range(16):
                psr = PS_SC[:, (hc % 2) * 1024:(hc % 2) * 1024 + 1024]
                pst = tgr("PSSC", (hc % 2) * 1024, (hc % 2) * 1024 + 1024)
                for tc2 in range(2):
                    tc = half * 2 + tc2
                    for kc in range(8):
                        mm(psr[:, tc2 * 512:(tc2 + 1) * 512],
                           AB(W1OF, 16384)[:, hc * 1024 + kc * 128: hc * 1024 + kc * 128 + 128],
                           TT[:, kc * 2048 + tc * 512: kc * 2048 + (tc + 1) * 512],
                           start=(kc == 0), stop=(kc == 7),
                           r=w1t + ttg(kc, tc * 4, tc * 4 + 4), w=pst)
                fn = AF.Gelu_apprx_tanh if gelu else AF.Silu
                bias = SM[:, 72 + hc:73 + hc] if b1 else None
                if cbc_src is None:
                    act(ATb[:, hc * 1024:(hc + 1) * 1024], psr, fn, bias=bias,
                        r=pst + (["SM"] if b1 else []), w=attag(hc * 1024, 1024))
                else:
                    silof = SILOF + 2048 + (hc % 2) * 1024
                    act(AB(silof, 1024), psr, fn, r=pst, w=atag(silof, 1024))
                    vop("pool", "tensor_tensor", ATb[:, hc * 1024:(hc + 1) * 1024],
                        AB(silof, 1024), AB(SILOF + 4096, 1024), ALU.mult,
                        r=atag(silof, 1024) + atag(SILOF + 4096, 1024),
                        w=attag(hc * 1024, 1024))
            for tl in range(8):
                tt = half * 8 + tl
                for oc in range(2):
                    ps = PS_A if (tl * 2 + oc) % 2 == 0 else PS_B
                    pst = ["PSA"] if (tl * 2 + oc) % 2 == 0 else ["PSB"]
                    for hc in range(16):
                        mm(ps[:, 0:512],
                           ATb[:, hc * 1024 + tl * 128: hc * 1024 + tl * 128 + 128],
                           AB(W2OF, 16384)[:, hc * 1024 + oc * 512: hc * 1024 + oc * 512 + 512],
                           start=(hc == 0), stop=(b2row is None and hc == 15),
                           r=attag(hc * 1024 + tl * 128, 128) + w2t, w=pst)
                    if b2row is not None:
                        mm(ps[:, 0:512], ONES1[:], b2row[0:1, oc * 512:(oc + 1) * 512],
                           start=False, stop=True, r=["ONES1", "B2ROW"], w=pst)
                    xr = xtg(tt * 1024 + oc * 512, tt * 1024 + oc * 512 + 512)
                    vop("dve", "tensor_tensor",
                        X[:, tt * 1024 + oc * 512: tt * 1024 + oc * 512 + 512],
                        X[:, tt * 1024 + oc * 512: tt * 1024 + oc * 512 + 512],
                        ps[:, 0:512], ALU.add, r=pst + xr, w=xr)

    def ttq(kc, tc, tt=None):
        if tc is not None:
            return TT[:, kc * 2048 + tc * 512: kc * 2048 + (tc + 1) * 512]
        return TT[:, kc * 2048 + tt * 128: kc * 2048 + tt * 128 + 128]

    def ttq_t(kc, tc, tt=None):
        if tc is not None:
            return ttg(kc, tc * 4, tc * 4 + 4)
        return ttg(kc, tt, tt + 1)

    def sublayer_attn_a():
        norm_to_TT("ln2", lambda tau: X[:, tau * 1024:(tau + 1) * 1024],
                   lambda tau: xtg(tau * 1024, tau * 1024 + 1024), NRMC_A, NRMY_A)
        for c in range(8):
            sset = c % 2
            qkv_pair(c, sset, dr["wq_a"], dr["wk_a"], dr["wv_a"], 8, 8,
                     ttq, ttq_t, ttq, ttq_t, qk_beta=True)
            prev = [None]
            for it in range(NT):
                for hh in range(2):
                    psel = it * 2 + hh

                    def evac(ps_att, rr, attw, sta, c=c, hh=hh, it=it):
                        vop("dve", "tensor_single_scalar", CMB[:, hh * 64:hh * 64 + 64],
                            ps_att, rr, ALU.mult, r=attw + sta, w=["CMB"])
                        if hh == 1:
                            gb = it % 2
                            trans(TRV[:, gb * 1024 + 512: gb * 1024 + 640], CMB[:, 0:128],
                                  r=["CMB", "IDN"], w=ptrtag(gb * 1024, gb * 1024 + 1024))
                            vop("dve", "tensor_copy",
                                ATb[:, c * 2048 + it * 128: c * 2048 + it * 128 + 128],
                                TRV[:, gb * 1024 + 512: gb * 1024 + 640],
                                r=ptrtag(gb * 1024, gb * 1024 + 1024),
                                w=attag(c * 2048 + it * 128, 128))
                    dfr = attention_step(QOF[sset] + (0 if hh == 0 else 2048),
                                         QOF[sset] + 4096 + (0 if hh == 0 else 2048),
                                         VOF[sset], hh, it, 4, None, (it == 0), psel, evac)
                    if prev[0] is not None:
                        prev[0]()
                    prev[0] = dfr
            if prev[0] is not None:
                prev[0]()
                prev[0] = None
        wo_phase(dr["wo_a"], BOROW, True)
        build_bcast(0, SILOF)
        build_bcast(1, SILOF + 1024)
        norm_to_TT("lnaff", lambda tau: AB(HUOF, 16384)[:, tau * 1024:(tau + 1) * 1024],
                   lambda tau: atag(tau * 1024, 1024), NRMC_F, NRMY_F,
                   gbc=(AB(SILOF, 1024), AB(SILOF + 1024, 1024)),
                   gbc_tags=atag(SILOF, 2048))
        mlp_halves(dr["w1_a"], dr["w2_a"], True, True, B2ROW, None)

    def sublayer_ffn():
        build_bcast(2, SILOF)
        norm_to_TT("lnrms", lambda tau: X[:, tau * 1024:(tau + 1) * 1024],
                   lambda tau: xtg(tau * 1024, tau * 1024 + 1024), NRMC_A, NRMY_A,
                   gbc=(AB(SILOF, 1024),), gbc_tags=atag(SILOF, 1024))
        dma("gd", AB(WSOF[0], 4096).rearrange("p (lc k) -> p lc k", lc=4),
            dr["wdkv"].rearrange("(lc p) k -> p lc k", p=128), w=atag(WSOF[0], 4096))
        for lc in range(4):
            for tc in range(4):
                for kc in range(8):
                    mm(PS_SC[:, tc * 512:(tc + 1) * 512],
                       AB(WSOF[0], 4096)[:, lc * 1024 + kc * 128: lc * 1024 + kc * 128 + 128],
                       TT[:, kc * 2048 + tc * 512: kc * 2048 + (tc + 1) * 512],
                       start=(kc == 0), stop=(kc == 7),
                       r=atag(WSOF[0], 4096) + ttg(kc, tc * 4, tc * 4 + 4),
                       w=tgr("PSSC", tc * 512, tc * 512 + 512))
            vop("dve", "tensor_copy", AB(CTOF + lc * 2048, 2048), PS_SC[:],
                r=tgr("PSSC", 0, 2048), w=atag(CTOF + lc * 2048, 2048))

        def ctq(kc, tc, tt=None):
            if tc is not None:
                return AB(CTOF + kc * 2048, 2048)[:, tc * 512:(tc + 1) * 512]
            return AB(CTOF + kc * 2048, 2048)[:, tt * 128:tt * 128 + 128]

        def ctq_t(kc, tc, tt=None):
            return atag(CTOF + kc * 2048, 2048)

        for c in range(8):
            sset = c % 2
            qkv_pair(c, sset, dr["wq_f"], dr["wuk"], dr["wuv"], 8, 4,
                     ttq, ttq_t, ctq, ctq_t, qscale=0.125)
            prev = [None]
            for it in range(NT):
                for hh in range(2):
                    psel = it * 2 + hh

                    def evac(ps_att, rr, attw, sta, c=c, hh=hh, it=it):
                        vop("dve", "tensor_single_scalar", CMB[:, hh * 64:hh * 64 + 64],
                            ps_att, rr, ALU.mult, r=attw + sta, w=["CMB"])
                        if hh == 1:
                            gb = it % 2
                            trans(TRV[:, gb * 1024 + 512: gb * 1024 + 640], CMB[:, 0:128],
                                  r=["CMB", "IDN"], w=ptrtag(gb * 1024, gb * 1024 + 1024))
                            vop("dve", "tensor_copy",
                                ATb[:, c * 2048 + it * 128: c * 2048 + it * 128 + 128],
                                TRV[:, gb * 1024 + 512: gb * 1024 + 640],
                                r=ptrtag(gb * 1024, gb * 1024 + 1024),
                                w=attag(c * 2048 + it * 128, 128))
                    dfr = attention_step(QOF[sset] + (0 if hh == 0 else 2048),
                                         QOF[sset] + 4096 + (0 if hh == 0 else 2048),
                                         VOF[sset], hh, it, it // 4 + 1, it % 4, False,
                                         psel, evac)
                    if prev[0] is not None:
                        prev[0]()
                    prev[0] = dfr
            if prev[0] is not None:
                prev[0]()
                prev[0] = None
        wo_phase(dr["wo_f"], None, True)
        build_bcast(3, SILOF)
        norm_to_TT("rmsg", lambda tau: AB(HUOF, 16384)[:, tau * 1024:(tau + 1) * 1024],
                   lambda tau: atag(tau * 1024, 1024), NRMC_F, NRMY_F,
                   gbc=(AB(SILOF, 1024),), gbc_tags=atag(SILOF, 1024))
        for tt in range(NT):
            for kc in range(8):
                mm(PS_A[:, 0:8], TT[:, kc * 2048 + tt * 128: kc * 2048 + tt * 128 + 128],
                   GWS[:, kc * 8:(kc + 1) * 8], start=(kc == 0), stop=(kc == 7),
                   r=ttg(kc, tt, tt + 1) + ["GWS"], w=["PSA"])
            stg = ["STg"]
            lg = ST[:, 36:44]
            m1 = ST[:, 44:45]
            z8 = ST[:, 45:53]
            i1 = ST[:, 53:61]
            v2 = ST[:, 61:62]
            rD = ST[:, 62:63]
            ez = ST[:, 36:44]
            vop("dve", "tensor_copy", lg, PS_A[:, 0:8], r=["PSA"], w=stg)
            vop("dve", "reduce_max", m1, lg, AX.X, r=stg, w=stg)
            vop("dve", "tensor_single_scalar", z8, lg, m1, ALU.subtract, r=stg, w=stg)
            vop("dve", "tensor_single_scalar", i1, z8, -1e-20, ALU.is_ge, r=stg, w=stg)
            vop("dve", "scalar_tensor_tensor", lg, i1, -1e9, z8, ALU.mult, ALU.add,
                r=stg, w=stg)
            vop("dve", "reduce_max", v2, lg, AX.X, r=stg, w=stg)
            vop("dve", "tensor_single_scalar", lg, lg, v2, ALU.subtract, r=stg, w=stg)
            vop("dve", "tensor_single_scalar", lg, lg, -1e-20, ALU.is_ge, r=stg, w=stg)
            vop("dve", "tensor_tensor", i1, i1, lg, ALU.add, r=stg, w=stg)
            ev2 = ST[:, 63:64]
            act(ez, z8, AF.Exp, r=stg, w=stg)
            act(ev2, v2, AF.Exp, r=stg, w=stg)
            vop("dve", "tensor_single_scalar", ev2, ev2, 1.0, ALU.add, r=stg, w=stg)
            vop("dve", "reciprocal", rD, ev2, r=stg, w=stg)
            vop("dve", "tensor_tensor", ez, ez, i1, ALU.mult, r=stg, w=stg)
            vop("dve", "tensor_single_scalar", CMB[:, 0:8], ez, rD, ALU.mult, r=stg,
                w=["CMB"])
            gb = tt % 2
            trans(TRV[0:8, gb * 1024: gb * 1024 + 128], CMB[:, 0:8], r=["CMB", "IDN"],
                  w=ptrtag(gb * 1024, gb * 1024 + 1024))
            vop("dve", "tensor_copy", CT8[0:8, tt * 128:(tt + 1) * 128],
                TRV[0:8, gb * 1024: gb * 1024 + 128],
                r=ptrtag(gb * 1024, gb * 1024 + 1024), w=["CT8"])
        for e in range(NEXP):
            dma("sd", STGE[0:1, :], CT8[e:e + 1, :], r=["CT8"], w=["STGE"])
            mlp_halves(dr["w1_m"][e * 2048:(e + 1) * 2048, :],
                       dr["w2_m"][e * 2048:(e + 1) * 2048, :],
                       False, False, None, STGE)

    def sublayer_attn_b(qkv_only=False, npairs=8):
        norm_to_TT("ln", lambda tau: X[:, tau * 1024:(tau + 1) * 1024],
                   lambda tau: xtg(tau * 1024, tau * 1024 + 1024), NRMC_A, NRMY_A)
        for c in range(npairs):
            sset = c % 2
            qkv_pair(c, sset, dr["wq_b"], dr["wk_b"], dr["wv_b"], 8, 8,
                     ttq, ttq_t, ttq, ttq_t, qscale=0.125)
            if qkv_only:
                continue
            for hh in range(2):
                qof = QOF[sset] + (0 if hh == 0 else 2048)
                kof = QOF[sset] + 4096 + (0 if hh == 0 else 2048)
                vof = VOF[sset]
                qtags = atag(qof, 2048)
                ktags = atag(kof, 2048)
                vtags = atag(vof, 2048)
                h = 2 * c + hh
                for w0 in range(0, NT, 2):
                    for wi in range(2):
                        w = w0 + wi
                        jlo = max(0, (w - 1) * 128)
                        jhi = min(S, (w + 2) * 128)
                        nj = jhi - jlo
                        pof = wi * 1024
                        rows = slice(0, 64) if wi == 0 else slice(64, 128)
                        mm(PS_SC[:, pof:pof + nj],
                           AB(qof, 2048)[rows, w * 128:(w + 1) * 128],
                           AB(kof, 2048)[rows, jlo:jhi], True, True,
                           tp=(0, 0) if wi == 0 else (64, 0),
                           r=qtags + ktags, w=tgr("PSSC", pof, pof + nj))
                    for wi in range(2):
                        w = w0 + wi
                        jlo = max(0, (w - 1) * 128)
                        jhi = min(S, (w + 2) * 128)
                        nj = jhi - jlo
                        njt = nj // 128
                        pof = wi * 1024
                        scw = tgr("PSSC", pof, pof + nj)
                        sta = ["STa%d" % wi]
                        nm = ST[:, 10 + wi:11 + wi]
                        ssum = ST[:, 20 + wi:21 + wi]
                        rr = ST[:, 22 + wi:23 + wi]
                        vop("dve", "reduce_max", nm,
                            PS_SC[:, pof:pof + nj], AX.X, None, None, True, r=scw, w=sta)
                        psof = PSOF[wi]
                        act(AB(psof, 512)[:, 0:nj], PS_SC[:, pof:pof + nj], AF.Exp,
                            bias=nm, accum=ssum, r=scw + sta, w=atag(psof, 512) + sta)
                        vop("dve", "reciprocal", rr, ssum, r=sta, w=sta)
                        for q in range(njt):
                            trans(TRV[:, wi * 1024 + q * 128: wi * 1024 + (q + 1) * 128],
                                  AB(psof, 512)[:, q * 128:(q + 1) * 128],
                                  r=atag(psof, 512) + ["IDN"],
                                  w=ptrtag(wi * 1024, wi * 1024 + 1024))
                        ptof = PTOF[wi]
                        vop("dve", "tensor_copy",
                            AB(ptof, 512)[:, 0:njt * 128],
                            TRV[:, wi * 1024:wi * 1024 + njt * 128],
                            r=ptrtag(wi * 1024, wi * 1024 + 1024), w=atag(ptof, 512))
                        ps_att = (PS_A if wi == 0 else PS_B)[:, 0:64]
                        attw = ["PSA" if wi == 0 else "PSB"]
                        for q in range(njt):
                            jt = jlo // 128 + q
                            mm(ps_att, AB(ptof, 512)[:, q * 128:(q + 1) * 128],
                               AB(vof, 2048)[:, jt * 128 + hh * 64: jt * 128 + hh * 64 + 64],
                               start=(q == 0), stop=(q == njt - 1),
                               r=atag(ptof, 512) + vtags, w=attw)
                        xr = xtg(w * 1024 + h * 64, w * 1024 + h * 64 + 64)
                        vop("dve", "scalar_tensor_tensor",
                            X[:, w * 1024 + h * 64: w * 1024 + h * 64 + 64],
                            ps_att, rr, X[:, w * 1024 + h * 64: w * 1024 + h * 64 + 64],
                            ALU.mult, ALU.add, r=attw + sta + xr, w=xr)

    if "n1" in PARTS:
        norm_to_TT("ln2", lambda tau: X[:, tau * 1024:(tau + 1) * 1024],
                   lambda tau: xtg(tau * 1024, tau * 1024 + 1024), NRMC_A, NRMY_A)
    if "nl" in PARTS:
        norm_to_TT("ln", lambda tau: X[:, tau * 1024:(tau + 1) * 1024],
                   lambda tau: xtg(tau * 1024, tau * 1024 + 1024), NRMC_A, NRMY_A)
    if any(isinstance(q, str) and q.startswith("nt") for q in PARTS):
        stage = [q for q in PARTS if isinstance(q, str)][0]
        tau = 0
        rot = 0
        Cof = NRMC_A
        yof = NRMY_A
        YH = AB(yof, 1024)
        ytags = atag(yof, 1024)
        C, ctags, rs, t = ln_tile(X[:, 0:1024], xtg(0, 1024), Cof, rot, 1e-5, "STn")
        if stage >= "ntB":
            vop("dve", "tensor_single_scalar", YH, C, rs, ALU.mult, r=ctags + t, w=ytags)
        if stage >= "ntC":
            for c in range(NC):
                g = c % 8
                trw = TRV[:, g * 128:(g + 1) * 128]
                trans(trw, YH[:, c * 128:(c + 1) * 128], r=ytags + ["IDN"],
                      w=ptrtag(g * 128, g * 128 + 128))
                if stage >= "ntD":
                    vop("dve", "tensor_copy",
                        TT[:, c * 2048 + tau * 128: c * 2048 + tau * 128 + 128], trw,
                        r=ptrtag(g * 128, g * 128 + 128), w=["TT%d_%d" % (c, tau)])
    if 1 in PARTS:
        sublayer_attn_a()
    if 2 in PARTS:
        sublayer_ffn()
    if 3 in PARTS:
        sublayer_attn_b()
    if "3q" in PARTS:
        sublayer_attn_b(qkv_only=True, npairs=2)
    if "3a" in PARTS:
        sublayer_attn_b(npairs=1)
    if "3b" in PARTS:
        sublayer_attn_b(npairs=2)
    if "3c" in PARTS:
        sublayer_attn_b(npairs=6)
    if 4 in PARTS:
        sublayer_ffn()

    dma("gd", out, X[:], r=xtg(0, 16384), w=["OUT"])

    final = {s: p.cnt[s] for s in STREAMS}

    def replay(section, eng):
        for st, waits, fn in p.items[section]:
            for s2, q in waits:
                eng.wait_ge(sems[s2], q * INC[s2])
            fn(eng).then_inc(sems[st], INC[st])

    with es:
        with nc.Block() as block:
            @block.sync
            def _(eng):
                replay("sync", eng)
                for s in STREAMS:
                    if final[s]:
                        eng.wait_ge(sems[s], final[s] * INC[s])

            @block.gpsimd
            def _(eng):
                replay("gpsimd", eng)

            @block.vector
            def _(eng):
                replay("vector", eng)

            @block.scalar
            def _(eng):
                replay("scalar", eng)

            @block.tensor
            def _(eng):
                replay("tensor", eng)

    return nc


_CACHE = {}


def kernel(**inputs):
    d = _repack(inputs)
    key = "prog" + str(PARTS)
    if key not in _CACHE:
        _CACHE[key] = build_program()
    nc = _CACHE[key]
    shared = {k: v for k, v in d.items() if k != "x"}
    in_maps = [dict(shared, x=d["x"][b]) for b in range(8)]
    res = run_bass_kernel_spmd(nc, in_maps, core_ids=list(range(8)))
    outs = []
    for b in range(8):
        o = np.asarray(res.results[b]["out"]).reshape(128, NT, D).transpose(1, 0, 2)
        outs.append(o.reshape(S, D))
    return np.stack(outs).astype(np.float32)
